# version 01
# baseline (speedup 1.0000x reference)
import math
from contextlib import ExitStack

import numpy as np
import ml_dtypes

import concourse.bass as bass
import concourse.mybir as mybir
from concourse.bass_core_rust import IndirectOffsetOnAxis
from concourse.bass_utils import run_bass_kernel_spmd

F32 = mybir.dt.float32
BF16 = mybir.dt.bfloat16
I32 = mybir.dt.int32
AF = mybir.ActivationFunctionType
ALU = mybir.AluOpType
AX = mybir.AxisListType

D = 2048
S = 2048
NSEQ = 2
NT = 16
NC = 16
EPS = 1e-5

ENGINES = ("tensor", "vector", "scalar", "sync", "gpsimd")


class _Op:
    __slots__ = ("idx", "eng", "emit", "deps", "is_dma", "group", "seq",
                 "needs_inc", "tag", "waits")

    def __init__(self, idx, eng, emit, deps, is_dma, group, tag):
        self.idx = idx
        self.eng = eng
        self.emit = emit
        self.deps = deps
        self.is_dma = is_dma
        self.group = group
        self.seq = 0
        self.needs_inc = False
        self.tag = tag


class Prog:
    def __init__(self):
        self.ops = []
        self.last_write = {}
        self.readers = {}
        self.group_counts = {}
        self.phase = 0

    def op(self, eng, emit, reads=(), writes=(), dma_group=None, n_dma=1,
           tag="", small=False):
        if small:
            self.op(eng, emit, reads=reads, writes=writes, tag=tag)
            junk = self.junk

            def padfn(e):
                if eng == "scalar":
                    return e.memzero(junk[:, :])
                return e.memset(junk[:, :], 0.0)
            return self.op(eng, padfn, reads=(), writes=writes, tag=tag + "_f")
        idx = len(self.ops)
        deps = set()
        reads = list(reads) + [("PH", self.phase)]
        for r in reads:
            w = self.last_write.get(r)
            if w is not None:
                deps.add(w)
        for w in writes:
            lw = self.last_write.get(w)
            if lw is not None:
                deps.add(lw)
            for rd in self.readers.get(w, ()):
                deps.add(rd)
        deps.discard(idx)
        o = _Op(idx, eng, emit, deps, dma_group is not None, dma_group, tag)
        if dma_group is not None:
            n = self.group_counts.get(dma_group, 0) + n_dma
            self.group_counts[dma_group] = n
            o.seq = n
        self.ops.append(o)
        for r in reads:
            self.readers.setdefault(r, []).append(idx)
        for w in writes:
            self.last_write[w] = idx
            self.readers[w] = []
        return idx

    def join(self, junk):
        old = self.phase
        self.phase += 1
        self.op("vector", lambda e: e.memset(junk[:, :], 0.0),
                reads=[], writes=[("PH", self.phase), ("PH", old)],
                tag=f"join{self.phase}")

    def finalize(self, nc, ctx):
        ops = self.ops
        for o in ops:
            for d in o.deps:
                p = ops[d]
                if p.is_dma or p.eng != o.eng:
                    p.needs_inc = True
        eng_prefix = {e: {} for e in ENGINES}
        counts = {e: 0 for e in ENGINES}
        for o in ops:
            if (not o.is_dma) and o.needs_inc:
                counts[o.eng] += 1
            eng_prefix[o.eng][o.idx] = counts[o.eng]
        eng_sem = {e: ctx.enter_context(nc.semaphore(f"se_{e}")) for e in ENGINES}
        grp_sem = {g: ctx.enter_context(nc.semaphore(f"sd{i}"))
                   for i, g in enumerate(self.group_counts)}
        for o in ops:
            waits = {}
            for d in o.deps:
                p = ops[d]
                if p.is_dma:
                    sem, v = grp_sem[p.group], 16 * p.seq
                elif p.eng != o.eng:
                    sem, v = eng_sem[p.eng], eng_prefix[p.eng][p.idx]
                    if v == 0:
                        continue
                else:
                    continue
                if waits.get(sem, 0) < v:
                    waits[sem] = v
            o.waits = waits
        seen = {e: {} for e in ENGINES}
        for o in ops:
            s = seen[o.eng]
            o.waits = {sem: v for sem, v in o.waits.items() if s.get(sem, -1) < v}
            for sem, v in o.waits.items():
                s[sem] = v
        self.eng_sem = eng_sem
        self.grp_sem = grp_sem

    def emit_into(self, block, final_groups, junk):
        ops = self.ops
        grp_sem = self.grp_sem
        eng_sem = self.eng_sem

        def run(eng_name):
            def body(eng):
                recent = [None, None]
                pad_engines = ("vector", "scalar", "gpsimd")

                def padop():
                    if eng_name == "scalar":
                        eng.memzero(junk[:, :])
                    else:
                        eng.memset(junk[:, :], 0.0)
                for o in ops:
                    if o.eng != eng_name:
                        continue
                    if eng_name in pad_engines and not o.is_dma:
                        npad = 0
                        if recent[0] is not None and recent[0] in o.deps:
                            npad = 2
                        elif recent[1] is not None and recent[1] in o.deps:
                            npad = 1
                        for _ in range(npad):
                            padop()
                            recent = [None, recent[0]]
                    for sem, v in sorted(o.waits.items(), key=lambda kv: kv[0].num):
                        eng.wait_ge(sem, v)
                    if o.is_dma:
                        o.emit(eng, grp_sem[o.group])
                    else:
                        ins = o.emit(eng)
                        if o.needs_inc:
                            ins.then_inc(eng_sem[o.eng], 1)
                    recent = [o.idx, recent[0]]
                if eng_name == "vector":
                    for g in final_groups:
                        eng.wait_ge(grp_sem[g], 16 * self.group_counts[g])
            return body
        block.tensor(run("tensor"))
        block.vector(run("vector"))
        block.scalar(run("scalar"))
        block.sync(run("sync"))
        block.gpsimd(run("gpsimd"))


def dap(t, offset, *dims):
    return bass.AP(t, offset, [list(d) for d in dims])


def host_constants():
    c = {}
    c["ident_f32"] = np.eye(128, dtype=np.float32)
    m = np.zeros((128, 128), dtype=np.float32)
    m[np.triu_indices(128, 1)] = -1.0e30
    c["causal_mask"] = m
    inv = (1.0 / (10000.0 ** (np.arange(0, 64, 2, dtype=np.float32) / 64.0)))
    ang = np.arange(S, dtype=np.float32)[:, None] * inv[None, :]
    cos = np.cos(ang).T.astype(np.float32)
    sin = np.sin(ang).T.astype(np.float32)
    c["rope_c"] = np.ascontiguousarray(np.concatenate([cos, cos, cos, cos], 0))
    c["rope_s"] = np.ascontiguousarray(np.concatenate([-sin, sin, -sin, sin], 0))
    lt = np.zeros((6, 6), dtype=np.float32)
    for e in range(6):
        lt[e, :e] = 1.0
    c["elt_mask"] = np.ascontiguousarray(lt.reshape(1, 36))
    c["iota128"] = np.arange(128, dtype=np.float32).reshape(128, 1)
    c["iota_row"] = np.arange(2048, dtype=np.float32).reshape(1, 2048)
    c["tokid"] = np.ascontiguousarray(
        (np.arange(32, dtype=np.int32)[None, :] * 128
         + np.arange(128, dtype=np.int32)[:, None]))
    perm = np.zeros((128, 128), dtype=np.float32)
    for blk in range(2):
        b0 = 64 * blk
        for i in range(32):
            perm[b0 + i, b0 + 32 + i] = 1.0
            perm[b0 + 32 + i, b0 + i] = 1.0
    c["rope_perm"] = np.ascontiguousarray(perm.T)
    return c


def relayout_weights(inp):
    w = {}
    w["wqkv0_t"] = np.ascontiguousarray(np.asarray(inp["L0_attn_w_qkv"]).T)
    w["wout0_t"] = np.ascontiguousarray(np.asarray(inp["L0_attn_w_out"]).T)
    wq1 = np.asarray(inp["L1_attn_w_qkv"]).T
    lo = list(range(0, 8)) + list(range(16, 24))
    hi = list(range(8, 16)) + list(range(24, 32))
    cols = []
    for c in range(16):
        cols.extend(range(64 * lo[c], 64 * lo[c] + 64))
        cols.extend(range(64 * hi[c], 64 * hi[c] + 64))
    cols.extend(range(2048, 2560))
    w["wqkv1_t"] = np.ascontiguousarray(wq1[:, cols])
    w["wo1_t"] = np.ascontiguousarray(np.asarray(inp["L1_attn_w_o"]).T)
    w["w_base"] = np.ascontiguousarray(np.asarray(inp["L0_ffn_W_base"]))
    w["a_s"] = np.ascontiguousarray(
        np.asarray(inp["L0_ffn_A_s"]).reshape(16, 128, 16).transpose(1, 0, 2)
        .reshape(128, 256))
    w["b_s"] = np.ascontiguousarray(np.asarray(inp["L0_ffn_B_s"]))
    w["a_r"] = np.ascontiguousarray(
        np.asarray(inp["L0_ffn_A_r"]).transpose(1, 0, 2).reshape(16, 128, 96)
        .transpose(1, 0, 2).reshape(128, 1536))
    w["b_r"] = np.ascontiguousarray(
        np.asarray(inp["L0_ffn_B_r"]).transpose(1, 0, 2).reshape(16, 6 * 2048))
    w["w_router"] = np.ascontiguousarray(
        np.asarray(inp["L0_ffn_W_router"]).reshape(16, 128, 6).transpose(1, 0, 2)
        .reshape(128, 96))
    w["b_router"] = np.ascontiguousarray(
        np.asarray(inp["L0_ffn_b_router"]).reshape(1, 6))
    w["b_base"] = np.ascontiguousarray(
        np.asarray(inp["L0_ffn_b_base"]).reshape(16, 128))
    w["wr1"] = np.ascontiguousarray(np.asarray(inp["L1_ffn_Wr"]).reshape(1, 2048))
    w["br1"] = np.ascontiguousarray(np.asarray(inp["L1_ffn_br"]).reshape(1, 1))
    w["w1"] = np.ascontiguousarray(np.asarray(inp["L1_ffn_W1"]))
    w["b1"] = np.ascontiguousarray(np.asarray(inp["L1_ffn_b1"]).reshape(64, 128))
    w["w2"] = np.ascontiguousarray(np.asarray(inp["L1_ffn_W2"]))
    w["b2"] = np.ascontiguousarray(np.asarray(inp["L1_ffn_b2"]).reshape(16, 128))
    return w


KB = 1024


class Mem:
    def __init__(self, nc):
        self.nc = nc
        base = nc.sbuf_base
        base = (base + 31) // 32 * 32
        self.PERS = base
        self.WS = base + 10 * KB
        self.A = base + 28 * KB
        self.B = self.A + 64 * KB
        self.C = self.B + 64 * KB
        self.END = self.C + 48 * KB
        assert self.END <= nc.sbuf_top, (self.END, nc.sbuf_top)
        self._pers_off = self.PERS
        self._ws_off = self.WS

    def pers(self, name, shape, dt):
        t = self.nc.alloc_sbuf_tensor_at(name, list(shape), dt,
                                         offset=self._pers_off)
        nb = shape[-1] if len(shape) == 2 else int(np.prod(shape[1:]))
        esz = 2 if dt == BF16 else 4
        self._pers_off += (nb * esz + 31) // 32 * 32
        assert self._pers_off <= self.WS, "PERS overflow"
        return t

    def ws(self, name, shape, dt):
        t = self.nc.alloc_sbuf_tensor_at(name, list(shape), dt,
                                         offset=self._ws_off)
        nb = int(np.prod(shape[1:]))
        esz = 2 if dt == BF16 else 4
        self._ws_off += (nb * esz + 31) // 32 * 32
        assert self._ws_off <= self.A, "WS overflow"
        return t

    def at(self, region_off, byte_off, name, shape, dt):
        nb = int(np.prod(shape[1:]))
        esz = 2 if dt == BF16 else 4
        assert region_off + byte_off + nb * esz <= self.END
        return self.nc.alloc_sbuf_tensor_at(name, list(shape), dt,
                                            offset=region_off + byte_off)


class MK:
    def __init__(self, nc, ctx, phases, dbg=False):
        self.dbg = dbg
        self.nc = nc
        self.ctx = ctx
        self.P = Prog()
        self.mem = Mem(nc)
        self.phases_sel = phases
        self.final_groups = set()
        self.uid = 0

    def u(self, base):
        self.uid += 1
        return f"{base}_{self.uid}"

    def build(self, input_names):
        nc = self.nc
        self.d = {}
        for name, (shape, np_dt) in input_names.items():
            dt = F32 if np_dt == np.float32 else I32
            self.d[name] = nc.dram_tensor(name, list(shape), dt,
                                          kind="ExternalInput")
        self.x_out = nc.dram_tensor("x_out", [NSEQ, S, D], F32,
                                    kind="ExternalOutput")
        s = {}
        kind = "ExternalOutput" if self.dbg else "Internal"
        s["xbufA"] = nc.dram_tensor("xbufA", [NSEQ, S, D], F32, kind="Internal")
        s["xbufB"] = nc.dram_tensor("xbufB", [NSEQ, S, D], F32, kind="Internal")
        s["lnt_d"] = nc.dram_tensor("lnt_d", [D, S], BF16, kind=kind)
        s["q0_d"] = nc.dram_tensor("q0_d", [D, S], BF16, kind=kind)
        s["comb_d"] = nc.dram_tensor("comb_d", [D, S], BF16, kind=kind)
        s["h_full"] = nc.dram_tensor("h_full", [NSEQ * S, D], F32, kind=kind)
        s["row_d"] = nc.dram_tensor("row_d", [576, 2048], F32, kind=kind)
        s["rw_d"] = nc.dram_tensor("rw_d", [NSEQ, S], F32, kind=kind)
        s["tok_d"] = nc.dram_tensor("tok_d", [NSEQ * 256], I32, kind=kind)
        self.s = s

        self.alloc_pers()
        chain = [("l0_attn", self.l0_attn),
                 ("l0_ffn", self.l0_ffn),
                 ("l1_attn", self.l1_attn),
                 ("l1_ffn", self.l1_ffn)]
        chain = [(n, f) for n, f in chain if n in self.phases_sel]
        srcs = [self.d["L0_attn_x"]]
        bufs = [s["xbufA"], s["xbufB"]]
        for i, (n, f) in enumerate(chain):
            dst = self.x_out if i == len(chain) - 1 else bufs[i % 2]
            f(srcs[-1], dst)
            srcs.append(dst)
            self.P.join(self.junk)

        self.P.finalize(nc, self.ctx)
        with nc.Block() as block:
            self.P.emit_into(block, sorted(self.final_groups), self.junk)

    def alloc_pers(self):
        m = self.mem
        self.P.junk = None
        self.ident_b = m.pers("ident_b", [128, 128], BF16)
        self.ident_f = m.pers("ident_f", [128, 128], F32)
        self.cmask = m.pers("cmask", [128, 128], F32)
        self.junk = m.pers("junk", [128, 64], F32)
        self.P.junk = self.junk
        st = {}
        for n in ("rowsum", "negmu", "ssq", "lnv", "rstd", "negmax",
                  "ebias", "rsum", "rcp", "mrs", "eps", "t1", "t2", "t3"):
            st[n] = m.pers(f"st_{n}", [128, 1], F32)
        self.st = st
        self.P.op("vector", lambda e: e.memset(st["eps"][:, :], EPS),
                  writes=[("st", "eps")])
        P = self.P
        d = self.d
        P.op("gpsimd", lambda e, q: e.dma_start(self.ident_b[:, :], d["ident_f32"].ap()).then_inc(q, 16),
             writes=[("ident_b",)], dma_group="cst_id")
        P.op("sync", lambda e, q: e.dma_start(self.ident_f[:, :], d["ident_f32"].ap()).then_inc(q, 16),
             writes=[("ident_f",)], dma_group="cst_if")
        P.op("sync", lambda e, q: e.dma_start(self.cmask[:, :], d["causal_mask"].ap()).then_inc(q, 16),
             writes=[("cmask",)], dma_group="cst_cm")
        self.ps_big = self.ctx.enter_context(
            self.nc.psum_tensor("ps_big", [128, 2048], F32))
        self.ps_t = self.ctx.enter_context(
            self.nc.psum_tensor("ps_t", [128, 2048], BF16))
        self.ps_aux = self.ctx.enter_context(
            self.nc.psum_tensor("ps_aux", [128, 1024], F32))

    def ln_stats(self, yreads, y_ap, sq_ap, sq_res, sum_done=False):
        P, st = self.P, self.st
        n = y_ap.shape[-1]
        if not sum_done:
            P.op("vector", lambda e: e.reduce_sum(st["rowsum"][:, :], y_ap, axis=AX.X),
                 reads=list(yreads), writes=[("st", "rowsum")])
        P.op("vector", lambda e: e.tensor_scalar_mul(st["negmu"][:, :],
                                                     st["rowsum"][:, :], -1.0 / n),
             reads=[("st", "rowsum")], writes=[("st", "negmu")])
        self.pad("vector")
        P.op("scalar", lambda e: e.activation(sq_ap, y_ap, AF.Square,
                                              bias=st["negmu"][:, :], scale=1.0,
                                              accum_out=st["ssq"][:, :]),
             reads=list(yreads) + [("st", "negmu")],
             writes=[sq_res, ("st", "ssq")])
        P.op("scalar", lambda e: e.activation(st["lnv"][:, :], st["ssq"][:, :],
                                              AF.Ln, bias=st["eps"][:, :],
                                              scale=1.0 / n),
             reads=[("st", "ssq"), ("st", "eps")], writes=[("st", "lnv")])
        P.op("scalar", lambda e: e.activation(st["rstd"][:, :], st["lnv"][:, :],
                                              AF.Exp, bias=0.0, scale=-0.5),
             reads=[("st", "lnv")], writes=[("st", "rstd")])
        self.pad("scalar")

    def pad(self, eng):
        if eng == "scalar":
            self.P.op(eng, lambda e: e.memzero(self.junk[:, :]), tag="pad")
        else:
            self.P.op(eng, lambda e: e.memset(self.junk[:, :], 0.0), tag="pad")

    def l0_attn(self, src, dst):
        nc, P, m, st, d, s = self.nc, self.P, self.mem, self.st, self.d, self.s
        scale = 1.0 / math.sqrt(128.0)
        u = self.u

        xa = m.at(m.WS, 0, u("xa"), [128, D], F32)
        xb = m.at(m.WS, 8 * KB, u("xb"), [128, D], F32)
        yout = m.at(m.C, 40 * KB, u("yout"), [128, D], F32)
        stg = m.at(m.WS, 16 * KB, u("stg"), [128, 512], BF16)
        vstg = m.at(m.WS, 17 * KB, u("vstg"), [128, 512], BF16)

        kbuf = m.at(m.A, 0, u("kbuf"), [128, NC, S], BF16)
        vbuf = m.at(m.B, 0, u("vbuf"), [128, NT, D], BF16)

        for sq in range(NSEQ):
            lnt_tile = m.at(m.C, 0, u("lnt_tile"), [128, D], BF16)
            for t in range(NT):
                xt = xa if t % 2 == 0 else xb
                xres = ("xt", t % 2)
                P.op("sync", lambda e, q, sq=sq, t=t, xt=xt: e.dma_start(
                    xt[:, :], src.ap()[sq, 128 * t:128 * (t + 1), :]).then_inc(q, 16),
                    writes=[xres], dma_group=f"p1x{t % 2}")
                self.ln_stats([xres], xt[:, :], yout[:, :], ("yout", 0))
                P.op("vector", lambda e, xt=xt: e.tensor_scalar(
                    lnt_tile[:, :], xt[:, :], st["negmu"][:, :], st["rstd"][:, :],
                    op0=ALU.add, op1=ALU.mult),
                    reads=[xres, ("st", "negmu"), ("st", "rstd")],
                    writes=[("lnt_tile",)], tag="ln_apply")
                for jg in range(4):
                    base = 1024 * (jg % 2)
                    def tr(e, jg=jg, base=base):
                        for j4 in range(4):
                            j = 4 * jg + j4
                            ins = e.matmul(
                                self.ps_t[:, base + 128 * j4:base + 128 * (j4 + 1)],
                                lnt_tile[:, 128 * j:128 * (j + 1)],
                                self.ident_b[:, :],
                                start=(j4 == 0),
                                stop=True, is_transpose=True)
                        return ins
                    P.op("tensor", tr, reads=[("lnt_tile",), ("ident_b",)],
                         writes=[("ps_t", jg % 2)], tag="lnT_tr")
                    P.op("scalar", lambda e, base=base: e.copy(
                        stg[:, :], self.ps_t[:, base:base + 512]),
                        reads=[("ps_t", jg % 2)], writes=[("stg",)])
                    P.op("sync", lambda e, q, jg=jg, t=t: e.dma_start(
                        dap(s["lnt_d"], (512 * jg) * S + 128 * t,
                            (S, 128), (128 * S, 4), (1, 128)),
                        dap(stg, 0, (512, 128), (128, 4), (1, 128))).then_inc(q, 16),
                        reads=[("stg",)], writes=[("lnt_d", jg, t // 4)],
                        dma_group="lnt_st")
            self.P.join(self.junk)

            wband = [m.at(m.C, 0, u("wband"), [128, NC, 256], BF16),
                     m.at(m.C, 8 * KB, u("wband"), [128, NC, 256], BF16)]
            lslc = [m.at(m.C, 16 * KB, u("lslc"), [128, NC, 512], BF16),
                    m.at(m.C, 32 * KB, u("lslc"), [128, NC, 512], BF16)]
            for g in range(24):
                wb = wband[g % 2]
                def wload(e, q, g=g, wb=wb):
                    for j in range(NC):
                        e.dma_start(
                            wb[:, j, :],
                            dap(d["wqkv0_t"], (128 * j) * 6144 + 256 * g,
                                (6144, 128), (1, 256))).then_inc(q, 16)
                P.op("gpsimd", wload, writes=[("wband", g % 2)],
                     dma_group=f"wband{g % 2}", n_dma=NC)
                for tb in range(4):
                    ls = lslc[tb % 2]
                    lres = ("lslc", tb % 2)
                    def lload(e, q, tb=tb, ls=ls):
                        for j in range(NC):
                            e.dma_start(
                                ls[:, j, :],
                                dap(s["lnt_d"], (128 * j) * S + 512 * tb,
                                    (S, 128), (1, 512))).then_inc(q, 16)
                    P.op("sync", lload,
                         reads=[("lnt_d", jg2, tb) for jg2 in range(4)],
                         writes=[lres], dma_group=f"lslc{tb % 2}", n_dma=NC)
                    for mm in range(2):
                        sl = 2 * (tb % 2) + mm
                        def gemm(e, mm=mm, sl=sl, wb=wb, ls=ls):
                            for k in range(NC):
                                ins = e.matmul(
                                    self.ps_big[:, 512 * sl:512 * (sl + 1)],
                                    wb[:, k, 128 * mm:128 * (mm + 1)],
                                    ls[:, k, :], start=(k == 0), stop=(k == NC - 1))
                            return ins
                        P.op("tensor", gemm, reads=[("wband", g % 2), lres],
                             writes=[("ps_big", sl)], tag="qkv_mm")
                        oc = 2 * g + mm
                        if 16 <= oc < 32:
                            kc = oc - 16
                            P.op("scalar", lambda e, sl=sl, kc=kc, tb=tb: e.copy(
                                kbuf[:, kc, 512 * tb:512 * (tb + 1)],
                                self.ps_big[:, 512 * sl:512 * (sl + 1)]),
                                reads=[("ps_big", sl)], writes=[("kbuf", kc, tb)])
                        elif oc >= 32:
                            vc = oc - 32
                            P.op("scalar", lambda e, sl=sl: e.copy(
                                vstg[:, :], self.ps_big[:, 512 * sl:512 * (sl + 1)]),
                                reads=[("ps_big", sl)], writes=[("vstg",)],
                                small=True)
                            def vtr(e):
                                for uu in range(4):
                                    ins = e.matmul(
                                        self.ps_t[:, 128 * uu:128 * (uu + 1)],
                                        vstg[:, 128 * uu:128 * (uu + 1)],
                                        self.ident_b[:, :],
                                        start=(uu == 0), stop=True,
                                        is_transpose=True)
                                return ins
                            P.op("tensor", vtr, reads=[("vstg",), ("ident_b",)],
                                 writes=[("ps_t", 0)], tag="v_tr")
                            P.op("vector", lambda e, vc=vc, tb=tb: e.tensor_copy(
                                dap(vbuf, (4 * tb) * D + 128 * vc,
                                    (NT * D, 128), (D, 4), (1, 128)),
                                self.ps_t[:, 0:512]),
                                reads=[("ps_t", 0)],
                                writes=[("vbuf", vc, tb)], tag="v_evac")
                        else:
                            qc = oc
                            P.op("scalar", lambda e, sl=sl: e.copy(
                                stg[:, :], self.ps_big[:, 512 * sl:512 * (sl + 1)]),
                                reads=[("ps_big", sl)], writes=[("stg",)])
                            P.op("sync", lambda e, q, qc=qc, tb=tb: e.dma_start(
                                dap(s["q0_d"], (128 * qc) * S + 512 * tb,
                                    (S, 128), (1, 512)),
                                stg[:, :]).then_inc(q, 16),
                                reads=[("stg",)], writes=[("q0_d", qc, tb)],
                                dma_group="q0_st")

            self.P.join(self.junk)
            qblk = m.at(m.C, 0, u("qblk"), [128, NC, 512], BF16)
            combblk = m.at(m.C, 16 * KB, u("combblk"), [128, NC, 512], BF16)
            psb = m.at(m.C, 32 * KB, u("psb"), [128, S], BF16)
            ptb = m.at(m.C, 36 * KB, u("ptb"), [128, S], BF16)
            for tb in range(4):
                def qload(e, q, tb=tb):
                    for j in range(NC):
                        e.dma_start(
                            qblk[:, j, :],
                            dap(s["q0_d"], (128 * j) * S + 512 * tb,
                                (S, 128), (1, 512))).then_inc(q, 16)
                P.op("sync", qload,
                     reads=[("q0_d", qc, tb) for qc in range(16)],
                     writes=[("qblk",)], dma_group="qblk", n_dma=NC)
                for uu in range(4):
                    for h in range(16):
                        def smm(e, h=h, uu=uu):
                            for kb in range(4):
                                ins = e.matmul(
                                    self.ps_big[:, 512 * kb:512 * (kb + 1)],
                                    qblk[:, h, 128 * uu:128 * (uu + 1)],
                                    kbuf[:, h, 512 * kb:512 * (kb + 1)],
                                    start=True, stop=True)
                            return ins
                        P.op("tensor", smm,
                             reads=[("qblk",)] + [("kbuf", h, kb2) for kb2 in range(4)],
                             writes=[("ps_big", 0), ("ps_big", 1),
                                     ("ps_big", 2), ("ps_big", 3)], tag="score")
                        P.op("vector", lambda e: e.reduce_max(
                            st["negmax"][:, :], self.ps_big[:, :], axis=AX.X,
                            negate=True),
                            reads=[("ps_big", 0), ("ps_big", 1),
                                   ("ps_big", 2), ("ps_big", 3)],
                            writes=[("st", "negmax")])
                        P.op("vector", lambda e: e.tensor_scalar_mul(
                            st["ebias"][:, :], st["negmax"][:, :], scale),
                            reads=[("st", "negmax")], writes=[("st", "ebias")])
                        self.pad("vector")
                        P.op("scalar", lambda e: e.activation(
                            psb[:, :], self.ps_big[:, :], AF.Exp,
                            bias=st["ebias"][:, :], scale=scale,
                            accum_out=st["rsum"][:, :]),
                            reads=[("ps_big", 0), ("ps_big", 1), ("ps_big", 2),
                                   ("ps_big", 3), ("st", "ebias")],
                            writes=[("psb",), ("st", "rsum")], tag="exp")
                        self.pad("scalar")
                        P.op("vector", lambda e: e.reciprocal(
                            st["rcp"][:, :], st["rsum"][:, :]),
                            reads=[("st", "rsum")], writes=[("st", "rcp")])
                        P.op("vector", lambda e: e.tensor_scalar(
                            psb[:, :], psb[:, :], st["rcp"][:, :], None,
                            op0=ALU.mult),
                            reads=[("st", "rcp"), ("psb",)], writes=[("psb",)],
                            tag="pnorm")
                        def ptr(e):
                            for c in range(16):
                                ins = e.matmul(
                                    self.ps_t[:, 128 * c:128 * (c + 1)],
                                    psb[:, 128 * c:128 * (c + 1)],
                                    self.ident_b[:, :],
                                    start=(c % 8 == 0), stop=True,
                                    is_transpose=True)
                            return ins
                        P.op("tensor", ptr, reads=[("psb",), ("ident_b",)],
                             writes=[("ps_t", 0), ("ps_t", 1)], tag="ptr")
                        P.op("scalar", lambda e: e.copy(ptb[:, :], self.ps_t[:, :]),
                             reads=[("ps_t", 0), ("ps_t", 1)],
                             writes=[("ptb",)], tag="pt_evac")
                        def avmm(e, h=h):
                            for c in range(16):
                                ins = e.matmul(
                                    self.ps_aux[:, 128 * (h % 4):128 * (h % 4 + 1)],
                                    vbuf[:, c, 128 * h:128 * (h + 1)],
                                    ptb[:, 128 * c:128 * (c + 1)],
                                    start=(c == 0 and h % 4 == 0), stop=(c == 15))
                            return ins
                        P.op("tensor", avmm,
                             reads=[("ptb",)] + [("vbuf", h, tb2) for tb2 in range(4)],
                             writes=[("ps_aux", 0, h % 4)], tag="av")
                        if h % 4 == 3:
                            hg = h // 4
                            P.op("vector", lambda e, hg=hg, uu=uu: e.tensor_copy(
                                dap(combblk, (4 * hg) * 512 + 128 * uu,
                                    (NC * 512, 128), (512, 4), (1, 128)),
                                self.ps_aux[:, 0:512]),
                                reads=[("ps_aux", 0, 0), ("ps_aux", 0, 1),
                                       ("ps_aux", 0, 2), ("ps_aux", 0, 3)],
                                writes=[("combblk", hg, uu)], tag="comb_evac")
                P.op("sync", lambda e, q, tb=tb: e.dma_start(
                    dap(s["comb_d"], 512 * tb, (S, 128), (128 * S, NC), (1, 512)),
                    combblk[:, :, :]).then_inc(q, 16),
                    reads=[("combblk", hh, uu2) for hh in range(4) for uu2 in range(4)],
                    writes=[("comb_d", tb)], dma_group="comb_st")

            self.P.join(self.junk)
            woband = m.at(m.A, 0, u("woband"), [128, NC, 2048], BF16)
            cblk2 = m.at(m.B, 0, u("cblk2"), [128, NC, 512], BF16)
            def woload(e, q):
                for j in range(NC):
                    e.dma_start(
                        woband[:, j, :],
                        dap(d["wout0_t"], (128 * j) * D, (D, 128), (1, D))).then_inc(q, 16)
            P.op("gpsimd", woload, writes=[("woband",)], dma_group="woband",
                 n_dma=NC)
            for tb in range(4):
                def cload(e, q, tb=tb):
                    for j in range(NC):
                        e.dma_start(
                            cblk2[:, j, :],
                            dap(s["comb_d"], (128 * j) * S + 512 * tb,
                                (S, 128), (1, 512))).then_inc(q, 16)
                P.op("sync", cload, reads=[("comb_d", tb)],
                     writes=[("cblk2",)], dma_group="cblk2", n_dma=NC)
                for uu in range(4):
                    t = 4 * tb + uu
                    for hs in range(4):
                        def omm(e, uu=uu, hs=hs):
                            for j in range(NC):
                                ins = e.matmul(
                                    self.ps_big[:, 512 * hs:512 * (hs + 1)],
                                    cblk2[:, j, 128 * uu:128 * (uu + 1)],
                                    woband[:, j, 512 * hs:512 * (hs + 1)],
                                    start=(j == 0), stop=(j == NC - 1))
                            return ins
                        P.op("tensor", omm, reads=[("cblk2",), ("woband",)],
                             writes=[("ps_big", hs)], tag="oproj")
                    for hs in range(4):
                        P.op("scalar", lambda e, hs=hs: e.copy(
                            yout[:, 512 * hs:512 * (hs + 1)],
                            self.ps_big[:, 512 * hs:512 * (hs + 1)]),
                            reads=[("ps_big", hs)], writes=[("yout", hs)])
                    xres = ("xt", t % 2)
                    xt = xa if t % 2 == 0 else xb
                    P.op("sync", lambda e, q, sq=sq, t=t, xt=xt: e.dma_start(
                        xt[:, :], src.ap()[sq, 128 * t:128 * (t + 1), :]).then_inc(q, 16),
                        writes=[xres], dma_group=f"p4x{t % 2}")
                    self.resid_store(xres, xt, psb, ("psb",),
                                     [("yout", i) for i in range(4)], yout,
                                     dst, sq, t, f"res0{t % 2}")
            self.P.join(self.junk)
        self.final_groups.add("res00")
        self.final_groups.add("res01")

    def resid_store(self, xres, xt, dump, dump_res, yreads, y_sb, dst, sq, t,
                    group, sum_done=False):
        P, st = self.P, self.st
        if len(y_sb.shape) == 2:
            y_ap = y_sb[:, :]
        else:
            y_ap = y_sb
        self.ln_stats(yreads, y_ap, dump[:, :], dump_res, sum_done=sum_done)
        P.op("vector", lambda e: e.tensor_tensor(
            st["mrs"][:, :], st["negmu"][:, :], st["rstd"][:, :], op=ALU.mult),
            reads=[("st", "negmu"), ("st", "rstd")], writes=[("st", "mrs")])
        P.op("vector", lambda e, xt=xt: e.tensor_scalar(
            xt[:, :], xt[:, :], st["mrs"][:, :], None, op0=ALU.add),
            reads=[xres, ("st", "mrs")], writes=[xres])
        P.op("vector", lambda e, xt=xt: e.scalar_tensor_tensor(
            xt[:, :], y_ap, st["rstd"][:, :], xt[:, :],
            op0=ALU.mult, op1=ALU.add),
            reads=list(yreads) + [xres, ("st", "rstd")], writes=[xres],
            tag="resid")
        P.op("sync", lambda e, q, sq=sq, t=t, xt=xt: e.dma_start(
            dst.ap()[sq, 128 * t:128 * (t + 1), :], xt[:, :]).then_inc(q, 16),
            reads=[xres], writes=[(group, sq, t)], dma_group=group)

    def copy_col(self, src_ap, src_res, name="t1"):
        st, P = self.st, self.P
        P.op("vector", lambda e: e.tensor_copy(st[name][:src_ap.shape[0], :], src_ap),
             reads=[src_res], writes=[("st", name)])
        return st[name]

    def l0_ffn(self, src, dst):
        nc, P, m, st, d, s = self.nc, self.P, self.mem, self.st, self.d, self.s
        u = self.u
        xa = m.at(m.WS, 0, u("xa"), [128, D], F32)
        xb = m.at(m.WS, 8 * KB, u("xb"), [128, D], F32)
        fstg = m.at(m.WS, 16 * KB, u("fstg"), [128, 512], F32)
        hfm = m.at(m.B, 0, u("hfm"), [128, NC, S], BF16)
        brb = m.at(m.A, 0, u("brb"), [16, 6, S], BF16)
        hwT = m.at(m.A, 24 * KB, u("hwT"), [16, 6, S], BF16)
        wband = [m.at(m.A, 48 * KB, u("wbb"), [128, NC, 128], BF16),
                 m.at(m.A, 52 * KB, u("wbb"), [128, NC, 128], BF16)]
        sfm = m.at(m.A, 56 * KB, u("sfm"), [16, S], BF16)
        bsb = m.at(m.A, 60 * KB, u("bsb"), [16, S], BF16)
        htile = m.at(m.C, 0, u("htile"), [128, D], BF16)
        arb = m.at(m.C, 0, u("arb"), [128, 1536], BF16)
        asb = m.at(m.C, 3 * KB, u("asb"), [128, 256], BF16)
        gex = m.at(m.C, 4 * KB, u("gex"), [128, 6], F32)
        gts = m.at(m.C, 4 * KB + 32, u("gts"), [128, 36], F32)
        eqs = m.at(m.C, 4 * KB + 192, u("eqs"), [128, 36], F32)
        rank6 = m.at(m.C, 4 * KB + 352, u("rank6"), [128, 6], F32)
        hwstg = m.at(m.C, 4 * KB + 384, u("hwstg"), [128, 16], BF16)
        brc = m.at(m.C, 4 * KB + 416, u("brc"), [128, 6], F32)
        ltm = m.at(m.C, 4 * KB + 448, u("ltm"), [128, 36], F32)
        bbs = m.at(m.C, 4 * KB + 608, u("bbs"), [128, 16], F32)
        wexp = m.at(m.C, 5 * KB, u("wexp"), [128, NT * 6], F32)
        ytm = m.at(m.C, 8 * KB, u("ytm"), [128, 4, D], F32)
        yout = m.at(m.C, 40 * KB, u("yout"), [128, D], F32)
        wrt = m.at(m.C, 5 * KB + 512, u("wrt"), [128, 96], BF16)

        P.op("gpsimd", lambda e, q: e.dma_start(wrt[:, :], d["w_router"].ap()).then_inc(q, 16),
             writes=[("wrt",)], dma_group="ffw3")
        P.op("gpsimd", lambda e, q: e.dma_start(brb[:, :, :], d["b_r"].ap()).then_inc(q, 16),
             writes=[("brb",)], dma_group="ffw4")
        P.op("gpsimd", lambda e, q: e.dma_start(bsb[:, :], d["b_s"].ap()).then_inc(q, 16),
             writes=[("bsb",)], dma_group="ffw8")
        P.op("sync", lambda e, q: e.dma_start(
            brc[:, :], dap(d["b_router"], 0, (0, 128), (1, 6))).then_inc(q, 16),
            writes=[("brc",)], dma_group="ffw5")
        P.op("sync", lambda e, q: e.dma_start(
            ltm[:, :], dap(d["elt_mask"], 0, (0, 128), (1, 36))).then_inc(q, 16),
            writes=[("ltm",)], dma_group="ffw6")
        def bbload(e, q):
            for j in range(NC):
                e.dma_start(bbs[:, j:j + 1],
                            dap(d["b_base"], 128 * j, (1, 128), (1, 1))).then_inc(q, 16)
        P.op("sync", bbload, writes=[("bbs",)], dma_group="ffw7", n_dma=NC)

        for sq in range(NSEQ):
            for t in range(NT):
                xt = xa if t % 2 == 0 else xb
                xres = ("xt", t % 2)
                P.op("sync", lambda e, q, sq=sq, t=t, xt=xt: e.dma_start(
                    xt[:, :], src.ap()[sq, 128 * t:128 * (t + 1), :]).then_inc(q, 16),
                    writes=[xres], dma_group=f"f1x{t % 2}")
                self.ln_stats([xres], xt[:, :], yout[:, :], ("yout", 0))
                P.op("vector", lambda e, xt=xt: e.tensor_scalar(
                    htile[:, :], xt[:, :], st["negmu"][:, :], st["rstd"][:, :],
                    op0=ALU.add, op1=ALU.mult),
                    reads=[xres, ("st", "negmu"), ("st", "rstd")],
                    writes=[("htile",)], tag="ln_apply")
                for jg in range(4):
                    base = 1024 * (jg % 2)
                    def tr(e, jg=jg, base=base):
                        for j4 in range(4):
                            j = 4 * jg + j4
                            ins = e.matmul(
                                self.ps_t[:, base + 128 * j4:base + 128 * (j4 + 1)],
                                htile[:, 128 * j:128 * (j + 1)],
                                self.ident_b[:, :], start=(j4 == 0),
                                stop=True, is_transpose=True)
                        return ins
                    P.op("tensor", tr, reads=[("htile",), ("ident_b",)],
                         writes=[("ps_t", jg % 2)], tag="h_tr")
                    P.op("scalar", lambda e, jg=jg, base=base, t=t: e.copy(
                        dap(hfm, (4 * jg) * S + 128 * t, (NC * S, 128), (S, 4), (1, 128)),
                        self.ps_t[:, base:base + 512]),
                        reads=[("ps_t", jg % 2)], writes=[("hfm", jg, t)])
                def rmm(e, t=t):
                    for j in range(NC):
                        ins = e.matmul(self.ps_aux[:, 512:518],
                                       hfm[:, j, 128 * t:128 * (t + 1)],
                                       wrt[:, 6 * j:6 * (j + 1)],
                                       start=(j == 0), stop=(j == NC - 1))
                    return ins
                P.op("tensor", rmm,
                     reads=[("hfm", jg2, t) for jg2 in range(4)] + [("wrt",)],
                     writes=[("ps_aux", "b7")], tag="router")
                P.op("vector", lambda e: e.tensor_tensor(
                    gex[:, :], self.ps_aux[:, 512:518], brc[:, :], op=ALU.add),
                    reads=[("ps_aux", "b7"), ("brc",)], writes=[("gex",)])
                P.op("vector", lambda e: e.reduce_max(
                    st["negmax"][:, :], gex[:, :], axis=AX.X, negate=True),
                    reads=[("gex",)], writes=[("st", "negmax")])
                self.pad("vector")
                P.op("scalar", lambda e: e.activation(
                    gex[:, :], gex[:, :], AF.Exp, bias=st["negmax"][:, :],
                    scale=1.0, accum_out=st["rsum"][:, :]),
                    reads=[("gex",), ("st", "negmax")],
                    writes=[("gex",), ("st", "rsum")])
                self.pad("scalar")
                P.op("vector", lambda e: e.reciprocal(st["rcp"][:, :], st["rsum"][:, :]),
                     reads=[("st", "rsum")], writes=[("st", "rcp")])
                self.pad("vector")
                P.op("vector", lambda e: e.tensor_scalar(
                    gex[:, :], gex[:, :], st["rcp"][:, :], None, op0=ALU.mult),
                    reads=[("gex",), ("st", "rcp")], writes=[("gex",)])
                vA = dap(gex, 0, (6, 128), (1, 6), (0, 6))
                vB = dap(gex, 0, (6, 128), (0, 6), (1, 6))
                P.op("vector", lambda e, vA=vA, vB=vB: e.tensor_tensor(
                    gts[:, :], vA, vB, op=ALU.is_lt),
                    reads=[("gex",)], writes=[("gts",)])
                P.op("vector", lambda e, vA=vA, vB=vB: e.tensor_tensor(
                    eqs[:, :], vA, vB, op=ALU.is_equal),
                    reads=[("gex",)], writes=[("eqs",)])
                P.op("vector", lambda e: e.tensor_tensor(
                    eqs[:, :], eqs[:, :], ltm[:, :], op=ALU.mult),
                    reads=[("eqs",), ("ltm",)], writes=[("eqs",)])
                P.op("vector", lambda e: e.tensor_tensor(
                    gts[:, :], gts[:, :], eqs[:, :], op=ALU.add),
                    reads=[("gts",), ("eqs",)], writes=[("gts",)])
                P.op("vector", lambda e: e.tensor_reduce(
                    rank6[:, :], dap(gts, 0, (36, 128), (6, 6), (1, 6)),
                    axis=AX.X, op=ALU.add),
                    reads=[("gts",)], writes=[("rank6",)])
                self.pad("vector")
                P.op("vector", lambda e: e.tensor_scalar(
                    rank6[:, :], rank6[:, :], 1.5, None, op0=ALU.is_lt),
                    reads=[("rank6",)], writes=[("rank6",)])
                self.pad("vector")
                P.op("vector", lambda e, t=t: e.tensor_tensor(
                    wexp[:, 6 * t:6 * (t + 1)], gex[:, :], rank6[:, :], op=ALU.mult),
                    reads=[("gex",), ("rank6",)], writes=[("wexp", t)], tag="wexp")
            if self.dbg and sq == 1:
                P.op("sync", lambda e, q: e.dma_start(
                    dap(s["row_d"], 8 * 2048, (96, 128), (1, 96)),
                    wexp[:, :]).then_inc(q, 16),
                    reads=[("wexp", tt) for tt in range(NT)],
                    writes=[("dbg1",)], dma_group="dbg1")
            self.P.join(self.junk)

            P.op("gpsimd", lambda e, q: e.dma_start(arb[:, :], d["a_r"].ap()).then_inc(q, 16),
                 writes=[("arb",)], dma_group="ffw2")
            P.op("gpsimd", lambda e, q: e.dma_start(asb[:, :], d["a_s"].ap()).then_inc(q, 16),
                 writes=[("asb",)], dma_group="ffw1")
            for tb in range(4):
                def smm(e, tb=tb):
                    for j in range(NC):
                        ins = e.matmul(self.ps_aux[0:16, 512:],
                                       asb[:, 16 * j:16 * (j + 1)],
                                       hfm[:, j, 512 * tb:512 * (tb + 1)],
                                       start=(j == 0), stop=(j == NC - 1))
                    return ins
                P.op("tensor", smm,
                     reads=[("hfm", jg2, tt) for jg2 in range(4)
                            for tt in range(4 * tb, 4 * tb + 4)] + [("asb",)],
                     writes=[("ps_aux", "b7")], tag="sfm_mm")
                P.op("scalar", lambda e, tb=tb: e.copy(
                    sfm[:, 512 * tb:512 * (tb + 1)], self.ps_aux[0:16, 512:]),
                    reads=[("ps_aux", "b7")], writes=[("sfm", tb)], small=True)
            for t in range(NT):
                def hemm(e, t=t):
                    for ee in range(6):
                        for j in range(NC):
                            ins = e.matmul(
                                self.ps_aux[:, 16 * ee:16 * (ee + 1)],
                                hfm[:, j, 128 * t:128 * (t + 1)],
                                arb[:, 96 * j + 16 * ee:96 * j + 16 * (ee + 1)],
                                start=(j == 0 and ee == 0), stop=(j == NC - 1))
                    return ins
                P.op("tensor", hemm,
                     reads=[("hfm", jg2, t) for jg2 in range(4)] + [("arb",)],
                     writes=[("ps_aux", "b6")], tag="he_mm")
                for ee in range(6):
                    self.copy_col(wexp[:, 6 * t + ee:6 * t + ee + 1],
                                  ("wexp", t), "t1")
                    self.pad("vector")
                    P.op("vector", lambda e, ee=ee: e.tensor_scalar(
                        hwstg[:, :], self.ps_aux[:, 16 * ee:16 * (ee + 1)],
                        st["t1"][:, :], None, op0=ALU.mult),
                        reads=[("ps_aux", "b6"), ("st", "t1")],
                        writes=[("hwstg",)], tag="he_scale", small=True)
                    P.op("tensor", lambda e: e.matmul(
                        self.ps_t[0:16, 0:128], hwstg[:, :], self.ident_b[:, :],
                        start=True, stop=True, is_transpose=True),
                        reads=[("hwstg",), ("ident_b",)], writes=[("ps_t", 0)],
                        tag="hw_tr")
                    P.op("scalar", lambda e, ee=ee, t=t: e.copy(
                        hwT[0:16, ee, 128 * t:128 * (t + 1)], self.ps_t[0:16, 0:128]),
                        reads=[("ps_t", 0)], writes=[("hwT", ee, t // 4)],
                        small=True)
            if self.dbg and sq == 1:
                def dbg2(e, q):
                    for ee in range(3):
                        e.dma_start(
                            dap(s["row_d"], (16 + 16 * ee) * 2048, (2048, 16), (1, 2048)),
                            hwT[:, ee, :]).then_inc(q, 16)
                P.op("gpsimd", dbg2,
                     reads=[("hwT", ee, tb) for ee in range(3) for tb in range(4)],
                     writes=[("dbg2",)], dma_group="dbg2", n_dma=3)
            self.P.join(self.junk)

            for tb in range(4):
                for fc in range(NC):
                    wb = wband[fc % 2]
                    def wload(e, q, fc=fc, wb=wb):
                        for j in range(NC):
                            e.dma_start(
                                wb[:, j, :],
                                dap(d["w_base"], (128 * j) * D + 128 * fc,
                                    (D, 128), (1, 128))).then_inc(q, 16)
                    P.op("gpsimd", wload, writes=[("wbb", fc % 2)],
                         dma_group=f"wbb{fc % 2}", n_dma=NC)
                    sl = fc % 4
                    def gmm(e, sl=sl, tb=tb, wb=wb, fc=fc):
                        for k in range(NC):
                            e.matmul(self.ps_big[:, 512 * sl:512 * (sl + 1)],
                                     wb[:, k, :],
                                     hfm[:, k, 512 * tb:512 * (tb + 1)],
                                     start=(k == 0), stop=False)
                        e.matmul(self.ps_big[:, 512 * sl:512 * (sl + 1)],
                                 bsb[:, 128 * fc:128 * (fc + 1)],
                                 sfm[:, 512 * tb:512 * (tb + 1)],
                                 start=False, stop=False)
                        for ee in range(6):
                            ins = e.matmul(
                                self.ps_big[:, 512 * sl:512 * (sl + 1)],
                                brb[:, ee, 128 * fc:128 * (fc + 1)],
                                hwT[:, ee, 512 * tb:512 * (tb + 1)],
                                start=False, stop=(ee == 5))
                        return ins
                    P.op("tensor", gmm,
                         reads=[("wbb", fc % 2), ("bsb",), ("brb",)]
                         + [("sfm", tb)]
                         + [("hwT", ee, tb) for ee in range(6)]
                         + [("hfm", jg2, tt) for jg2 in range(4)
                            for tt in range(4 * tb, 4 * tb + 4)],
                         writes=[("ps_big", sl)], tag="ffn_mm")
                    self.copy_col(bbs[:, fc:fc + 1], ("bbs",), "t2")
                    self.pad("vector")
                    P.op("vector", lambda e, sl=sl: e.scalar_tensor_tensor(
                        fstg[:, :], self.ps_big[:, 512 * sl:512 * (sl + 1)],
                        1.0, st["t2"].broadcast_to([128, 512])[:, :],
                        op0=ALU.mult, op1=ALU.add),
                        reads=[("ps_big", sl), ("st", "t2")],
                        writes=[("fstg",)], tag="ffn_evac", small=True)
                    def ftr(e):
                        for uu in range(4):
                            ins = e.matmul(
                                self.ps_aux[:, 128 * uu:128 * (uu + 1)],
                                fstg[:, 128 * uu:128 * (uu + 1)],
                                self.ident_f[:, :], start=(uu == 0),
                                stop=True, is_transpose=True)
                        return ins
                    P.op("tensor", ftr, reads=[("fstg",), ("ident_f",)],
                         writes=[("ps_aux", "b6")], tag="ffn_tr")
                    P.op("vector", lambda e, fc=fc: e.tensor_copy(
                        dap(ytm, 128 * fc, (4 * D, 128), (D, 4), (1, 128)),
                        self.ps_aux[:, 0:512]),
                        reads=[("ps_aux", "b6")], writes=[("ytm", fc)],
                        tag="ytm_evac")
                for uu in range(4):
                    t = 4 * tb + uu
                    xres = ("xt", t % 2)
                    xt = xa if t % 2 == 0 else xb
                    P.op("sync", lambda e, q, sq=sq, t=t, xt=xt: e.dma_start(
                        xt[:, :], src.ap()[sq, 128 * t:128 * (t + 1), :]).then_inc(q, 16),
                        writes=[xres], dma_group=f"f3x{t % 2}")
                    self.resid_store(xres, xt, yout, ("yout", 0),
                                     [("ytm", fc2) for fc2 in range(NC)],
                                     ytm[:, uu, :], dst, sq, t, f"res1{t % 2}")
            self.P.join(self.junk)
        self.final_groups.add("res10")
        self.final_groups.add("res11")

    def l1_attn(self, src, dst):
        nc, P, m, st, d, s = self.nc, self.P, self.mem, self.st, self.d, self.s
        u = self.u
        scale = 1.0 / 8.0
        xa = m.at(m.WS, 0, u("xa"), [128, D], F32)
        xb = m.at(m.WS, 8 * KB, u("xb"), [128, D], F32)
        stg = m.at(m.WS, 16 * KB, u("stg"), [128, 512], BF16)
        qbuf = m.at(m.A, 0, u("qbuf"), [128, NC, S], BF16)
        lnt = m.at(m.B, 0, u("lnt"), [128, NC, S], BF16)
        wo1 = m.at(m.B, 0, u("wo1"), [128, NC, S], BF16)
        ropec = m.at(m.C, 0, u("ropec"), [128, S], BF16)
        ropes = m.at(m.C, 4 * KB, u("ropes"), [128, S], BF16)
        yout = m.at(m.C, 0, u("yout"), [128, D], F32)
        htile = m.at(m.C, 8 * KB, u("htile"), [128, D], BF16)
        wband = [m.at(m.C, 12 * KB, u("wb1"), [128, NC, 256], BF16),
                 m.at(m.C, 20 * KB, u("wb1"), [128, NC, 256], BF16)]
        psb = m.at(m.C, 12 * KB, u("psb"), [128, S], BF16)
        ptb = m.at(m.C, 16 * KB, u("ptb"), [128, S], BF16)
        comb = m.at(m.C, 20 * KB, u("comb"), [128, NC, 128], BF16)
        permb = m.at(m.C, 24 * KB, u("permb"), [128, 128], BF16)
        rtmp = m.at(m.C, 24 * KB + 256, u("rtmp"), [128, 512], F32)
        ksb = m.at(m.C, 28 * KB, u("ksb"), [128, 2, S], BF16)
        vsb = m.at(m.C, 36 * KB, u("vsb"), [128, NT, 256], BF16)

        for sq in range(NSEQ):
            for t in range(NT):
                xt = xa if t % 2 == 0 else xb
                xres = ("xt", t % 2)
                P.op("sync", lambda e, q, sq=sq, t=t, xt=xt: e.dma_start(
                    xt[:, :], src.ap()[sq, 128 * t:128 * (t + 1), :]).then_inc(q, 16),
                    writes=[xres], dma_group=f"a1x{t % 2}")
                self.ln_stats([xres], xt[:, :], yout[:, :], ("yout", 0))
                P.op("vector", lambda e, xt=xt: e.tensor_scalar(
                    htile[:, :], xt[:, :], st["negmu"][:, :], st["rstd"][:, :],
                    op0=ALU.add, op1=ALU.mult),
                    reads=[xres, ("st", "negmu"), ("st", "rstd")],
                    writes=[("htile",)], tag="ln_apply")
                for jg in range(4):
                    base = 1024 * (jg % 2)
                    def tr(e, jg=jg, base=base):
                        for j4 in range(4):
                            j = 4 * jg + j4
                            ins = e.matmul(
                                self.ps_t[:, base + 128 * j4:base + 128 * (j4 + 1)],
                                htile[:, 128 * j:128 * (j + 1)],
                                self.ident_b[:, :], start=(j4 == 0),
                                stop=True, is_transpose=True)
                        return ins
                    P.op("tensor", tr, reads=[("htile",), ("ident_b",)],
                         writes=[("ps_t", jg % 2)], tag="lnT_tr")
                    P.op("scalar", lambda e, jg=jg, base=base, t=t: e.copy(
                        dap(lnt, (4 * jg) * S + 128 * t, (NC * S, 128), (S, 4), (1, 128)),
                        self.ps_t[:, base:base + 512]),
                        reads=[("ps_t", jg % 2)], writes=[("lnt", jg, t // 4)])
            self.P.join(self.junk)
            for g in range(10):
                wb = wband[g % 2]
                def wload(e, q, g=g, wb=wb):
                    for j in range(NC):
                        e.dma_start(
                            wb[:, j, :],
                            dap(d["wqkv1_t"], (128 * j) * 2560 + 256 * g,
                                (2560, 128), (1, 256))).then_inc(q, 16)
                P.op("gpsimd", wload, writes=[("wb1", g % 2)],
                     dma_group=f"awb{g % 2}", n_dma=NC)
                for tb in range(4):
                    for mm2 in range(2):
                        oc = 2 * g + mm2
                        sl = 2 * (tb % 2) + mm2
                        def gmm(e, mm2=mm2, sl=sl, tb=tb, wb=wb):
                            for k in range(NC):
                                ins = e.matmul(
                                    self.ps_big[:, 512 * sl:512 * (sl + 1)],
                                    wb[:, k, 128 * mm2:128 * (mm2 + 1)],
                                    lnt[:, k, 512 * tb:512 * (tb + 1)],
                                    start=(k == 0), stop=(k == NC - 1))
                            return ins
                        P.op("tensor", gmm,
                             reads=[("wb1", g % 2)]
                             + [("lnt", jg2, tb) for jg2 in range(4)],
                             writes=[("ps_big", sl)], tag="qkv1_mm")
                        if oc < 16:
                            P.op("scalar", lambda e, oc=oc, tb=tb, sl=sl: e.copy(
                                qbuf[:, oc, 512 * tb:512 * (tb + 1)],
                                self.ps_big[:, 512 * sl:512 * (sl + 1)]),
                                reads=[("ps_big", sl)], writes=[("qbuf", oc, tb)])
                        elif oc < 18:
                            kc = oc - 16
                            P.op("scalar", lambda e, kc=kc, tb=tb, sl=sl: e.copy(
                                ksb[:, kc, 512 * tb:512 * (tb + 1)],
                                self.ps_big[:, 512 * sl:512 * (sl + 1)]),
                                reads=[("ps_big", sl)], writes=[("ksb", kc, tb)])
                        else:
                            vc = oc - 18
                            P.op("scalar", lambda e, sl=sl: e.copy(
                                stg[:, :], self.ps_big[:, 512 * sl:512 * (sl + 1)]),
                                reads=[("ps_big", sl)], writes=[("stg",)],
                                small=True)
                            def vtr(e):
                                for uu in range(4):
                                    ins = e.matmul(
                                        self.ps_t[:, 128 * uu:128 * (uu + 1)],
                                        stg[:, 128 * uu:128 * (uu + 1)],
                                        self.ident_b[:, :], start=(uu == 0),
                                        stop=True, is_transpose=True)
                                return ins
                            P.op("tensor", vtr, reads=[("stg",), ("ident_b",)],
                                 writes=[("ps_t", 0)], tag="v_tr")
                            P.op("vector", lambda e, vc=vc, tb=tb: e.tensor_copy(
                                dap(vsb, (4 * tb) * 256 + 128 * vc,
                                    (NT * 256, 128), (256, 4), (1, 128)),
                                self.ps_t[:, 0:512]),
                                reads=[("ps_t", 0)], writes=[("vsb", vc, tb)],
                                tag="v_evac")
            self.P.join(self.junk)
            P.op("gpsimd", lambda e, q: e.dma_start(
                ropec[:, :], d["rope_c"].ap()).then_inc(q, 16),
                writes=[("ropec",)], dma_group="l1c1")
            P.op("gpsimd", lambda e, q: e.dma_start(
                ropes[:, :], d["rope_s"].ap()).then_inc(q, 16),
                writes=[("ropes",)], dma_group="l1c2")
            P.op("gpsimd", lambda e, q: e.dma_start(
                permb[:, :], d["rope_perm"].ap()).then_inc(q, 16),
                writes=[("permb",)], dma_group="l1c3")
            def rope_chunk(buf, c, src_keys, out_key):
                for nb in range(4):
                    def rmm(e, nb=nb, c=c, buf=buf):
                        return e.matmul(self.ps_aux[:, 512:],
                                        permb[:, :],
                                        buf[:, c, 512 * nb:512 * (nb + 1)],
                                        start=True, stop=True)
                    P.op("tensor", rmm, reads=[("permb",)] + src_keys + [out_key],
                         writes=[("ps_aux", "b7")], tag="rope_mm")
                    P.op("vector", lambda e, nb=nb: e.tensor_tensor(
                        rtmp[:, :], self.ps_aux[:, 512:],
                        ropes[:, 512 * nb:512 * (nb + 1)], op=ALU.mult),
                        reads=[("ps_aux", "b7"), ("ropes",)], writes=[("rtmp",)])
                    P.op("vector", lambda e, nb=nb, c=c, buf=buf: e.tensor_tensor(
                        buf[:, c, 512 * nb:512 * (nb + 1)],
                        buf[:, c, 512 * nb:512 * (nb + 1)],
                        ropec[:, 512 * nb:512 * (nb + 1)], op=ALU.mult),
                        reads=src_keys + [out_key, ("ropec",)], writes=[out_key])
                    P.op("vector", lambda e, nb=nb, c=c, buf=buf: e.tensor_tensor(
                        buf[:, c, 512 * nb:512 * (nb + 1)],
                        buf[:, c, 512 * nb:512 * (nb + 1)], rtmp[:, :], op=ALU.add),
                        reads=[out_key, ("rtmp",)], writes=[out_key])
            import os as _os
            if not _os.environ.get("MK_NO_ROPE"):
                for c in range(NC):
                    rope_chunk(qbuf, c, [("qbuf", c, tb) for tb in range(4)],
                               ("qrope", c))
                for c in range(2):
                    rope_chunk(ksb, c, [("ksb", c, tb) for tb in range(4)],
                               ("krope", c))
            else:
                for c in range(NC):
                    P.op("vector", lambda e: e.memset(self.junk[:, :], 0.0),
                         reads=[("qbuf", c, tb) for tb in range(4)],
                         writes=[("qrope", c)])
                for c in range(2):
                    P.op("vector", lambda e: e.memset(self.junk[:, :], 0.0),
                         reads=[("ksb", c, tb) for tb in range(4)],
                         writes=[("krope", c)])
            self.P.join(self.junk)
            def wo1load(e, q):
                for j in range(NC):
                    e.dma_start(
                        wo1[:, j, :],
                        dap(d["wo1_t"], (128 * j) * D, (D, 128), (1, D))).then_inc(q, 16)
            P.op("gpsimd", wo1load, writes=[("wo1",)], dma_group="wo1", n_dma=NC)
            for i in range(NT):
                nk = 128 * (i + 1)
                nkb = (nk + 511) // 512
                for h in range(32):
                    kvh = h // 8
                    kc, kbase = kvh // 2, 64 * (kvh % 2)
                    if kvh % 2 == 0:
                        qc = h if h < 8 else h - 8
                        qbase = 0
                    else:
                        qc = h - 8 if h < 16 else h - 16
                        qbase = 64
                    def smm(e, i=i, nk=nk, qc=qc, qbase=qbase, kc=kc, kbase=kbase):
                        for kb in range((nk + 511) // 512):
                            w = min(512, nk - 512 * kb)
                            ins = e.matmul(
                                self.ps_big[:, 512 * kb:512 * kb + w],
                                qbuf[qbase:qbase + 64, qc, 128 * i:128 * (i + 1)],
                                ksb[kbase:kbase + 64, kc, 512 * kb:512 * kb + w],
                                start=True, stop=True)
                        return ins
                    P.op("tensor", smm,
                         reads=[("qrope", qc), ("krope", kc)],
                         writes=[("ps_big", kb2) for kb2 in range(nkb)],
                         tag="score1")
                    P.op("vector", lambda e, i=i: e.tensor_tensor(
                        self.ps_big[:, 128 * i:128 * (i + 1)],
                        self.ps_big[:, 128 * i:128 * (i + 1)],
                        self.cmask[:, :], op=ALU.add),
                        reads=[("ps_big", (128 * i) // 512), ("cmask",)],
                        writes=[("ps_big", (128 * i) // 512)], tag="cmask")
                    self.pad("vector")
                    P.op("vector", lambda e, nk=nk, nkb=nkb: e.reduce_max(
                        st["negmax"][:, :], self.ps_big[:, 0:nk], axis=AX.X,
                        negate=True),
                        reads=[("ps_big", kb2) for kb2 in range(nkb)],
                        writes=[("st", "negmax")])
                    P.op("vector", lambda e: e.tensor_scalar_mul(
                        st["ebias"][:, :], st["negmax"][:, :], scale),
                        reads=[("st", "negmax")], writes=[("st", "ebias")])
                    self.pad("vector")
                    P.op("scalar", lambda e, nk=nk, nkb=nkb: e.activation(
                        psb[:, 0:nk], self.ps_big[:, 0:nk], AF.Exp,
                        bias=st["ebias"][:, :], scale=scale,
                        accum_out=st["rsum"][:, :]),
                        reads=[("ps_big", kb2) for kb2 in range(nkb)]
                        + [("st", "ebias")],
                        writes=[("psb",), ("st", "rsum")], tag="exp1")
                    self.pad("scalar")
                    P.op("vector", lambda e: e.reciprocal(st["rcp"][:, :],
                                                          st["rsum"][:, :]),
                         reads=[("st", "rsum")], writes=[("st", "rcp")])
                    self.pad("vector")
                    P.op("vector", lambda e, nk=nk: e.tensor_scalar(
                        psb[:, 0:nk], psb[:, 0:nk], st["rcp"][:, :], None,
                        op0=ALU.mult),
                        reads=[("st", "rcp"), ("psb",)], writes=[("psb",)],
                        tag="pnorm1")
                    def ptr(e, i=i):
                        for c in range(i + 1):
                            ins = e.matmul(
                                self.ps_t[:, 128 * c:128 * (c + 1)],
                                psb[:, 128 * c:128 * (c + 1)],
                                self.ident_b[:, :], start=(c % 8 == 0),
                                stop=True, is_transpose=True)
                        return ins
                    P.op("tensor", ptr, reads=[("psb",), ("ident_b",)],
                         writes=[("ps_t", 0), ("ps_t", 1)], tag="ptr1")
                    P.op("scalar", lambda e, nk=nk: e.copy(ptb[:, 0:nk],
                                                           self.ps_t[:, 0:nk]),
                         reads=[("ps_t", 0), ("ps_t", 1)], writes=[("ptb",)],
                         tag="pt_evac1")
                    def avmm(e, i=i, kvh=kvh):
                        for c in range(i + 1):
                            ins = e.matmul(
                                self.ps_aux[0:64, 512:640],
                                vsb[:, c, 64 * kvh:64 * (kvh + 1)],
                                ptb[:, 128 * c:128 * (c + 1)],
                                start=(c == 0), stop=(c == i))
                        return ins
                    P.op("tensor", avmm,
                         reads=[("ptb",)] + [("vsb", vc2, tb2)
                                             for vc2 in range(2) for tb2 in range(4)],
                         writes=[("ps_aux", "av")], tag="av1")
                    P.op("scalar", lambda e, h=h: e.copy(
                        comb[64 * (h % 2):64 * (h % 2) + 64, h // 2, :],
                        self.ps_aux[0:64, 512:640]),
                        reads=[("ps_aux", "av")],
                        writes=[("comb", h), ("ps_aux", "av")], tag="comb_ev1",
                        small=True)
                for hs in range(4):
                    def omm(e, hs=hs):
                        for j in range(NC):
                            ins = e.matmul(
                                self.ps_big[:, 512 * hs:512 * (hs + 1)],
                                comb[:, j, :], wo1[:, j, 512 * hs:512 * (hs + 1)],
                                start=(j == 0), stop=(j == NC - 1))
                        return ins
                    P.op("tensor", omm,
                         reads=[("comb", hh) for hh in range(32)] + [("wo1",)],
                         writes=[("ps_big", hs)], tag="oproj1")
                P.op("vector", lambda e: e.tensor_scalar(
                    yout[:, :], self.ps_big[:, :], 0.0, None, op0=ALU.add,
                    op1=ALU.add, accum_out=st["rowsum"][:, :]),
                    reads=[("ps_big", 0), ("ps_big", 1), ("ps_big", 2),
                           ("ps_big", 3), ("ropec",), ("ropes",)],
                    writes=[("yout", 0), ("st", "rowsum")], tag="oy1")
                if self.dbg and sq == 0 and i == 1:
                    P.op("gpsimd", lambda e, q: e.dma_start(
                        dap(s["row_d"], 64 * 2048, (2048, 128), (1, 2048)),
                        psb[:, :]).then_inc(q, 16),
                        reads=[("psb",)], writes=[("dbgp",)], dma_group="dbgp")
                    P.op("gpsimd", lambda e, q: e.dma_start(
                        dap(s["row_d"], 384 * 2048, (2048, 128), (1, 2048)),
                        qbuf[:, 15, :]).then_inc(q, 16),
                        reads=[("qrope", 15)], writes=[("dbgq",)], dma_group="dbgq")
                    P.op("gpsimd", lambda e, q: e.dma_start(
                        dap(s["row_d"], 448 * 2048, (2048, 128), (1, 2048)),
                        ksb[:, 1, :]).then_inc(q, 16),
                        reads=[("krope", 1)], writes=[("dbgk",)], dma_group="dbgk")
                    P.op("gpsimd", lambda e, q: e.dma_start(
                        dap(s["row_d"], 128 * 2048, (2048, 128), (128, 16), (1, 128)),
                        comb[:, :, :]).then_inc(q, 16),
                        reads=[("comb", hh) for hh in range(32)],
                        writes=[("dbgc",)], dma_group="dbgc")
                    P.op("sync", lambda e, q: e.dma_start(
                        dap(s["row_d"], 256 * 2048, (2048, 128), (1, 2048)),
                        yout[:, :]).then_inc(q, 16),
                        reads=[("yout", 0)], writes=[("dbgy",)], dma_group="dbgy")
                xres = ("xt", i % 2)
                xt = xa if i % 2 == 0 else xb
                P.op("sync", lambda e, q, sq=sq, i=i, xt=xt: e.dma_start(
                    xt[:, :], src.ap()[sq, 128 * i:128 * (i + 1), :]).then_inc(q, 16),
                    writes=[xres], dma_group=f"a3x{i % 2}")
                self.resid_store(xres, xt, htile, ("htile",), [("yout", 0)],
                                 yout, dst, sq, i, f"res2{i % 2}", sum_done=True)
            self.P.join(self.junk)
        self.final_groups.add("res20")
        self.final_groups.add("res21")

    def l1_ffn(self, src, dst):
        nc, P, m, st, d, s = self.nc, self.P, self.mem, self.st, self.d, self.s
        u = self.u
        xa = m.at(m.WS, 0, u("xa"), [128, D], F32)
        xb = m.at(m.WS, 8 * KB, u("xb"), [128, D], F32)
        dump = m.at(m.C, 0, u("dump"), [128, D], F32)
        yout = m.at(m.C, 8 * KB, u("yout"), [128, D], F32)
        wrb = m.at(m.C, 16 * KB, u("wrb"), [128, D], F32)
        lb = m.at(m.C, 16 * KB, u("lb"), [128, S], F32)
        lrow = m.at(m.C, 24 * KB, u("lrow"), [1, S], F32)
        scr = m.at(m.C, 32 * KB, u("scr"), [1, S], F32)
        scr2 = m.at(m.C, 8 * KB, u("scr2"), [1, S], F32)
        cdump = m.at(m.C, 40 * KB, u("cdump"), [1, S], F32)
        wband = [m.at(m.C, 16 * KB, u("w1b"), [128, NC, 512], BF16),
                 m.at(m.C, 32 * KB, u("w1b"), [128, NC, 512], BF16)]
        w2band = [m.at(m.C, 16 * KB, u("w2b"), [128, 64, 128], BF16),
                  m.at(m.C, 32 * KB, u("w2b"), [128, 64, 128], BF16)]
        fxc16 = m.at(m.C, 16 * KB, u("fxc16"), [128, D], BF16)
        pp = m.PERS + 4 * KB
        lgtm = nc.alloc_sbuf_tensor_at(u("lgtm"), [128, 16], F32, offset=pp + 0)
        lg16 = nc.alloc_sbuf_tensor_at(u("lg16"), [16, 128], F32, offset=pp + 64)
        brs = nc.alloc_sbuf_tensor_at(u("brs"), [128, 1], F32, offset=pp + 576)
        selw = nc.alloc_sbuf_tensor_at(u("selw"), [1, 8], F32, offset=pp + 608)
        top8 = nc.alloc_sbuf_tensor_at(u("top8"), [1, 8], F32, offset=pp + 640)
        cnt8 = nc.alloc_sbuf_tensor_at(u("cnt8"), [1, 8], F32, offset=pp + 672)
        tauv = nc.alloc_sbuf_tensor_at(u("tauv"), [128, 1], F32, offset=pp + 704)
        cnt = nc.alloc_sbuf_tensor_at(u("cnt"), [128, 1], F32, offset=pp + 736)
        cntT = nc.alloc_sbuf_tensor_at(u("cntT"), [1, 128], F32, offset=pp + 768)
        iot = nc.alloc_sbuf_tensor_at(u("iot"), [1, 128], F32, offset=pp + 1280)
        lob = nc.alloc_sbuf_tensor_at(u("lob"), [128, 2], F32, offset=pp + 1792)
        iota_col = nc.alloc_sbuf_tensor_at(u("iotac"), [128, 1], F32, offset=pp + 1824)
        tok_sb = nc.alloc_sbuf_tensor_at(u("tok_sb"), [128, 4], I32, offset=pp + 1856)
        dint = nc.alloc_sbuf_tensor_at(u("dint"), [128, 16], I32, offset=pp + 1888)
        dintf = nc.alloc_sbuf_tensor_at(u("dintf"), [128, 16], F32, offset=pp + 1952)
        tokid = nc.alloc_sbuf_tensor_at(u("tokid"), [128, 32], I32, offset=pp + 2016)
        rwc = nc.alloc_sbuf_tensor_at(u("rwc"), [128, 4], F32, offset=pp + 2144)
        b1s = nc.alloc_sbuf_tensor_at(u("b1s"), [128, 64], F32, offset=pp + 2176)
        b2s = nc.alloc_sbuf_tensor_at(u("b2s"), [128, 16], F32, offset=pp + 2432)
        b64 = nc.alloc_sbuf_tensor_at(u("b64"), [64, 128], F32, offset=pp + 2496)
        fxT = m.at(m.B, 0, u("fxT"), [128, NC, 512], BF16)
        out_l = m.at(m.B, 16 * KB, u("out_l"), [128, NC, 512], F32)
        hh = m.at(m.A, 0, u("hh"), [128, 64, 512], BF16)

        P.op("sync", lambda e, q: e.dma_start(
            wrb[:, :], dap(d["wr1"], 0, (0, 128), (1, 2048))).then_inc(q, 16),
            writes=[("wrb",)], dma_group="fw1")
        P.op("sync", lambda e, q: e.dma_start(
            brs[:, :], dap(d["br1"], 0, (0, 128), (1, 1))).then_inc(q, 16),
            writes=[("brs",)], dma_group="fw2")
        P.op("sync", lambda e, q: e.dma_start(
            iot[:, :], dap(d["iota_row"], 0, (2048, 1), (1, 128))).then_inc(q, 16),
            writes=[("iot",)], dma_group="fw3")
        P.op("gpsimd", lambda e, q: e.dma_start(
            tokid[:, :], d["tokid"].ap()).then_inc(q, 16),
            writes=[("tokid",)], dma_group="fw4")
        P.op("sync", lambda e, q: e.dma_start(
            iota_col[:, :], d["iota128"].ap()).then_inc(q, 16),
            writes=[("iota_col",)], dma_group="fw6")
        P.op("sync", lambda e, q: e.dma_start(b64[:, :], d["b1"].ap()).then_inc(q, 16),
             writes=[("b64",)], dma_group="fw5")
        P.op("tensor", lambda e: e.matmul(self.ps_aux[:, 0:64], b64[:, :],
                                          self.ident_f[0:64, 0:64], start=True,
                                          stop=True, is_transpose=True),
             reads=[("b64",), ("ident_f",)], writes=[("ps_aux", "b1")], tag="b1T")
        P.op("vector", lambda e: e.tensor_copy(b1s[:, :], self.ps_aux[:, 0:64]),
             reads=[("ps_aux", "b1")], writes=[("b1s",)])
        P.op("sync", lambda e, q: e.dma_start(b64[0:16, :], d["b2"].ap()).then_inc(q, 16),
             reads=[("ps_aux", "b1"), ("b1s",)], writes=[("b64",)], dma_group="fw5")
        P.op("tensor", lambda e: e.matmul(self.ps_aux[:, 128:144], b64[0:16, :],
                                          self.ident_f[0:16, 0:16], start=False,
                                          stop=True, is_transpose=True),
             reads=[("b64",), ("ident_f",)], writes=[("ps_aux", "b2")], tag="b2T")
        P.op("vector", lambda e: e.tensor_copy(b2s[:, :], self.ps_aux[:, 128:144]),
             reads=[("ps_aux", "b2")], writes=[("b2s",)])

        for sq in range(NSEQ):
            for t in range(NT):
                xt = xa if t % 2 == 0 else xb
                xres = ("xt", t % 2)
                P.op("sync", lambda e, q, sq=sq, t=t, xt=xt: e.dma_start(
                    xt[:, :], src.ap()[sq, 128 * t:128 * (t + 1), :]).then_inc(q, 16),
                    writes=[xres], dma_group=f"g1x{t % 2}")
                self.ln_stats([xres], xt[:, :], dump[:, :], ("dump",))
                P.op("vector", lambda e, xt=xt: e.tensor_scalar(
                    yout[:, :], xt[:, :], st["negmu"][:, :], st["rstd"][:, :],
                    op0=ALU.add, op1=ALU.mult),
                    reads=[xres, ("st", "negmu"), ("st", "rstd")],
                    writes=[("yout", 0)], tag="ln_apply")
                P.op("sync", lambda e, q, sq=sq, t=t: e.dma_start(
                    s["h_full"].ap()[sq * S + 128 * t:sq * S + 128 * (t + 1), :],
                    yout[:, :]).then_inc(q, 16),
                    reads=[("yout", 0)], writes=[("h_full", sq, t)],
                    dma_group="hst")
                P.op("vector", lambda e: e.scalar_tensor_tensor(
                    dump[:, :], yout[:, :], 1.0, wrb[:, :], op0=ALU.mult,
                    op1=ALU.mult, accum_out=st["t1"][:, :]),
                    reads=[("yout", 0), ("wrb",)],
                    writes=[("dump",), ("st", "t1")], tag="logit")
                self.pad("vector")
                P.op("vector", lambda e, t=t: e.tensor_scalar(
                    lgtm[:, t:t + 1], st["t1"][:, :], brs[:, :], None, op0=ALU.add),
                    reads=[("st", "t1"), ("brs",)], writes=[("lgtm", t)],
                    small=True)
            P.op("tensor", lambda e: e.matmul(
                self.ps_aux[0:16, 0:128], lgtm[:, :], self.ident_f[:, :],
                start=True, stop=True, is_transpose=True),
                reads=[("lgtm", tt) for tt in range(NT)] + [("ident_f",)],
                writes=[("ps_aux", "lg")], tag="lgT")
            P.op("vector", lambda e: e.tensor_copy(lg16[:, :], self.ps_aux[0:16, 0:128]),
                 reads=[("ps_aux", "lg")], writes=[("lg16",)])
            self.pad("vector")
            P.op("sync", lambda e, q, sq=sq: e.dma_start(
                dap(s["row_d"], sq * 2048, (128, 16), (1, 128)),
                lg16[:, :]).then_inc(q, 16),
                reads=[("lg16",)], writes=[("row_d", sq)], dma_group="lgst")
        self.P.join(self.junk)

        ROFF = 4 * 2048

        def sel_ss(name, emit, reads=(), writes=()):
            P.op("vector", emit, reads=reads, writes=writes, tag=name)
            self.pad("vector")

        for sq in range(NSEQ):
            lbr = ("lbrow",)
            P.op("sync", lambda e, q, sq=sq: e.dma_start(
                lrow[0:1, :], dap(s["row_d"], sq * 2048, (2048, 1), (1, 2048))).then_inc(q, 16),
                reads=[("row_d", sq)], writes=[lbr], dma_group="selL")
            P.op("sync", lambda e, q, sq=sq: e.dma_start(
                lb[:, :], dap(s["row_d"], sq * 2048, (0, 128), (1, 2048))).then_inc(q, 16),
                reads=[("row_d", sq)], writes=[("lb",)], dma_group="selB")
            sel_ss("mx", lambda e: e.reduce_max(selw[0:1, 3:4], lrow[0:1, :], axis=AX.X),
                   reads=[lbr], writes=[("selw", 3)])
            sel_ss("mn", lambda e: e.tensor_reduce(selw[0:1, 0:1], lrow[0:1, :],
                                                   axis=AX.X, op=ALU.min),
                   reads=[lbr], writes=[("selw", 0)])
            sel_ss("lo1", lambda e: e.tensor_scalar(selw[0:1, 0:1], selw[0:1, 0:1],
                                                    1.0, None, op0=ALU.subtract),
                   reads=[("selw", 0)], writes=[("selw", 0)])
            sel_ss("hi1", lambda e: e.tensor_scalar(selw[0:1, 3:4], selw[0:1, 3:4],
                                                    1.0, None, op0=ALU.add),
                   reads=[("selw", 3)], writes=[("selw", 3)])
            for rnd in range(4):
                sel_ss("st1", lambda e: e.tensor_tensor(
                    selw[0:1, 1:2], selw[0:1, 3:4], selw[0:1, 0:1], op=ALU.subtract),
                    reads=[("selw", 0), ("selw", 3)], writes=[("selw", 1)])
                sel_ss("st2", lambda e: e.tensor_scalar(
                    selw[0:1, 1:2], selw[0:1, 1:2], 1.0 / 128, None, op0=ALU.mult),
                    reads=[("selw", 1)], writes=[("selw", 1)])
                P.op("sync", lambda e, q: e.dma_start(
                    dap(s["row_d"], ROFF, (1, 1), (1, 2)),
                    selw[0:1, 0:2]).then_inc(q, 16),
                    reads=[("selw", 0), ("selw", 1)],
                    writes=[("selrt",)], dma_group="selrt")
                P.op("sync", lambda e, q: e.dma_start(
                    lob[:, :], dap(s["row_d"], ROFF, (0, 128), (1, 2))).then_inc(q, 16),
                    reads=[("selrt",)], writes=[("lob",)], dma_group="selbc")
                P.op("vector", lambda e: e.tensor_tensor(
                    tauv[:, :], iota_col[:, :], lob[:, 1:2], op=ALU.mult),
                    reads=[("lob",), ("iota_col",)], writes=[("tauv",)])
                sel_ss("tau2", lambda e: e.tensor_tensor(
                    tauv[:, :], tauv[:, :], lob[:, 0:1], op=ALU.add),
                    reads=[("tauv",), ("lob",)], writes=[("tauv",)])
                P.op("vector", lambda e, sq=sq: e.tensor_scalar(
                    dump[:, :], lb[:, :], tauv[:, :], None, op0=ALU.is_gt,
                    op1=ALU.add, accum_out=cnt[:, :]),
                    reads=[("lb",), ("tauv",)],
                    writes=[("dump",), ("cnt",)], tag="count", small=True)
                self.pad("vector")
                P.op("tensor", lambda e: e.matmul(
                    self.ps_aux[0:1, 0:128], cnt[:, :], self.ident_f[:, :],
                    start=True, stop=True, is_transpose=True),
                    reads=[("cnt",), ("ident_f",)], writes=[("ps_aux", "ct")],
                    tag="cntT")
                P.op("vector", lambda e: e.tensor_copy(cntT[:, :],
                                                       self.ps_aux[0:1, 0:128]),
                     reads=[("ps_aux", "ct")], writes=[("cntT",)])
                sel_ss("cm", lambda e: e.tensor_scalar(
                    cntT[:, :], cntT[:, :], 255.5, None, op0=ALU.is_gt),
                    reads=[("cntT",)], writes=[("cntT",)])
                sel_ss("ci", lambda e: e.tensor_tensor(
                    cntT[:, :], cntT[:, :], iot[:, :], op=ALU.mult),
                    reads=[("cntT",), ("iot",)], writes=[("cntT",)])
                sel_ss("pm", lambda e: e.reduce_max(st["t2"][0:1, :], cntT[:, :],
                                                    axis=AX.X),
                       reads=[("cntT",)], writes=[("st", "t2")])
                sel_ss("ps", lambda e: e.tensor_tensor(
                    st["t3"][0:1, :], st["t2"][0:1, :], selw[0:1, 1:2], op=ALU.mult),
                    reads=[("st", "t2"), ("selw", 1)], writes=[("st", "t3")])
                sel_ss("lo2", lambda e: e.tensor_tensor(
                    selw[0:1, 0:1], selw[0:1, 0:1], st["t3"][0:1, :], op=ALU.add),
                    reads=[("st", "t3"), ("selw", 0)], writes=[("selw", 0)])
                sel_ss("hi2", lambda e: e.tensor_tensor(
                    selw[0:1, 3:4], selw[0:1, 0:1], selw[0:1, 1:2], op=ALU.add),
                    reads=[("selw", 0), ("selw", 1)], writes=[("selw", 3)])
            sel_ss("bm1", lambda e: e.tensor_scalar(
                scr[0:1, :], lrow[0:1, :], selw[0:1, 0:1], None, op0=ALU.is_gt),
                reads=[lbr, ("selw", 0)], writes=[("scr",)])
            sel_ss("bm2", lambda e: e.tensor_scalar(
                scr2[0:1, :], lrow[0:1, :], selw[0:1, 3:4], None, op0=ALU.is_le),
                reads=[lbr, ("selw", 3)], writes=[("scr2",)])
            sel_ss("bm3", lambda e: e.tensor_tensor(
                scr[0:1, :], scr[0:1, :], scr2[0:1, :], op=ALU.mult),
                reads=[("scr",), ("scr2",)], writes=[("scr",)])
            sel_ss("cd1", lambda e: e.tensor_scalar(
                scr2[0:1, :], scr[0:1, :], 1.0, None, op0=ALU.subtract),
                reads=[("scr",)], writes=[("scr2",)])
            sel_ss("cd2", lambda e: e.tensor_tensor(
                scr[0:1, :], scr[0:1, :], lrow[0:1, :], op=ALU.mult),
                reads=[("scr",), lbr], writes=[("scr",)])
            sel_ss("cd3", lambda e: e.scalar_tensor_tensor(
                scr[0:1, :], scr2[0:1, :], 3.0e38, scr[0:1, :],
                op0=ALU.mult, op1=ALU.add),
                reads=[("scr",), ("scr2",)], writes=[("scr",)])
            sel_ss("t8", lambda e: e.max(out=top8[0:1, :], in_=scr[0:1, :]),
                   reads=[("scr",)], writes=[("top8",)])
            for j in range(8):
                self.copy_col(top8[0:1, j:j + 1], ("top8",), "t1")
                self.pad("vector")
                P.op("vector", lambda e: e.tensor_scalar(
                    scr2[0:1, :], lrow[0:1, :], st["t1"][0:1, :], None,
                    op0=ALU.is_gt, op1=ALU.add, accum_out=st["t2"][0:1, :]),
                    reads=[lbr, ("st", "t1")], writes=[("scr2",), ("st", "t2")])
                self.pad("vector")
                P.op("vector", lambda e, j=j: e.tensor_copy(
                    cnt8[0:1, j:j + 1], st["t2"][0:1, :]),
                    reads=[("st", "t2")], writes=[("cnt8", j)])
            sel_ss("vs1", lambda e: e.tensor_scalar(
                scr2[0:1, 0:8], cnt8[0:1, :], 255.5, None, op0=ALU.is_lt),
                reads=[("cnt8", jj) for jj in range(8)], writes=[("scr2",)])
            sel_ss("vs2", lambda e: e.tensor_tensor(
                top8[0:1, :], top8[0:1, :], scr2[0:1, 0:8], op=ALU.mult),
                reads=[("top8",), ("scr2",)], writes=[("top8",)])
            sel_ss("vs3", lambda e: e.tensor_scalar(
                scr2[0:1, 0:8], scr2[0:1, 0:8], 1.0, None, op0=ALU.subtract),
                reads=[("scr2",)], writes=[("scr2",)])
            sel_ss("vs4", lambda e: e.scalar_tensor_tensor(
                top8[0:1, :], scr2[0:1, 0:8], -3.0e38, top8[0:1, :],
                op0=ALU.mult, op1=ALU.add),
                reads=[("top8",), ("scr2",)], writes=[("top8",)])
            sel_ss("vs5", lambda e: e.tensor_reduce(
                st["t1"][0:1, :], top8[0:1, :], axis=AX.X, op=ALU.min),
                reads=[("top8",)], writes=[("st", "t1")])
            P.op("vector", lambda e: e.tensor_scalar(
                scr[0:1, :], lrow[0:1, :], st["t1"][0:1, :], None, op0=ALU.is_gt,
                op1=ALU.add, accum_out=st["t2"][0:1, :]),
                reads=[lbr, ("st", "t1")], writes=[("scr",), ("st", "t2")])
            self.pad("vector")
            sel_ss("nd", lambda e: e.tensor_scalar(
                st["t3"][0:1, :], st["t2"][0:1, :], -1.0, 256.0,
                op0=ALU.mult, op1=ALU.add),
                reads=[("st", "t2")], writes=[("st", "t3")])
            sel_ss("eq", lambda e: e.tensor_scalar(
                scr2[0:1, :], lrow[0:1, :], st["t1"][0:1, :], None,
                op0=ALU.is_equal),
                reads=[lbr, ("st", "t1")], writes=[("scr2",)])
            P.op("vector", lambda e: e.memset(cdump[0:1, :], 0.0),
                 writes=[("cdump",)])
            self.pad("vector")
            sel_ss("cs", lambda e: e.tensor_tensor_scan(
                dump[0:1, :], scr2[0:1, :], cdump[0:1, :], 0.0,
                op0=ALU.add, op1=ALU.add),
                reads=[("scr2",), ("cdump",)], writes=[("dump",)])
            sel_ss("le", lambda e: e.tensor_scalar(
                dump[0:1, :], dump[0:1, :], st["t3"][0:1, :], None, op0=ALU.is_le),
                reads=[("dump",), ("st", "t3")], writes=[("dump",)])
            sel_ss("se", lambda e: e.tensor_tensor(
                scr2[0:1, :], scr2[0:1, :], dump[0:1, :], op=ALU.mult),
                reads=[("scr2",), ("dump",)], writes=[("scr2",)])
            sel_ss("mk", lambda e: e.tensor_tensor(
                scr[0:1, :], scr[0:1, :], scr2[0:1, :], op=ALU.add),
                reads=[("scr",), ("scr2",)], writes=[("scr",)])
            sel_ss("m1", lambda e: e.reduce_max(st["t2"][0:1, :], lrow[0:1, :],
                                                axis=AX.X, negate=True),
                   reads=[lbr], writes=[("st", "t2")])
            P.op("scalar", lambda e: e.activation(
                scr2[0:1, :], lrow[0:1, :], AF.Exp, bias=st["t2"][0:1, :],
                scale=1.0),
                reads=[lbr, ("st", "t2")], writes=[("scr2",)])
            self.pad("scalar")
            P.op("vector", lambda e: e.tensor_tensor(
                scr2[0:1, :], scr2[0:1, :], scr[0:1, :], op=ALU.mult),
                reads=[("scr2",), ("scr",)], writes=[("scr2",)])
            self.pad("vector")
            sel_ss("dn", lambda e: e.reduce_sum(st["t2"][0:1, :], scr2[0:1, :],
                                                axis=AX.X),
                   reads=[("scr2",)], writes=[("st", "t2")])
            sel_ss("dr", lambda e: e.reciprocal(st["t3"][0:1, :], st["t2"][0:1, :]),
                   reads=[("st", "t2")], writes=[("st", "t3")])
            sel_ss("rw", lambda e: e.tensor_scalar(
                scr2[0:1, :], scr2[0:1, :], st["t3"][0:1, :], None, op0=ALU.mult),
                reads=[("scr2",), ("st", "t3")], writes=[("scr2",)])
            P.op("sync", lambda e, q, sq=sq: e.dma_start(
                s["rw_d"].ap()[sq:sq + 1, :], scr2[0:1, :]).then_inc(q, 16),
                reads=[("scr2",)], writes=[("rw_d", sq)], dma_group="rwst")
            P.op("vector", lambda e: e.memset(cdump[0:1, :], 0.0),
                 writes=[("cdump",)])
            self.pad("vector")
            sel_ss("pc", lambda e: e.tensor_tensor_scan(
                dump[0:1, :], scr[0:1, :], cdump[0:1, :], 0.0,
                op0=ALU.add, op1=ALU.add),
                reads=[("scr",), ("cdump",)], writes=[("dump",)])
            sel_ss("pd", lambda e, sq=sq: e.tensor_scalar(
                dump[0:1, :], dump[0:1, :], -1.0 + 256.0 * sq, None, op0=ALU.add),
                reads=[("dump",)], writes=[("dump",)])
            sel_ss("pe", lambda e: e.tensor_tensor(
                dump[0:1, :], dump[0:1, :], scr[0:1, :], op=ALU.mult),
                reads=[("dump",), ("scr",)], writes=[("dump",)])
            sel_ss("pf", lambda e: e.tensor_scalar(
                scr[0:1, :], scr[0:1, :], -1.0e9, 1.0e9,
                op0=ALU.mult, op1=ALU.add),
                reads=[("scr",)], writes=[("scr",)])
            sel_ss("pg", lambda e: e.tensor_tensor(
                dump[0:1, :], dump[0:1, :], scr[0:1, :], op=ALU.add),
                reads=[("dump",), ("scr",)], writes=[("dump",)])
            P.op("sync", lambda e, q, sq=sq: e.dma_start(
                dap(s["row_d"], (5 + sq) * 2048, (2048, 1), (1, 2048)),
                dump[0:1, :]).then_inc(q, 16),
                reads=[("dump",)], writes=[("dest_d", sq)], dma_group="dstst")
            P.op("sync", lambda e, q, sq=sq: e.dma_start(
                lg16[:, :], dap(s["row_d"], (5 + sq) * 2048, (128, 16), (1, 128))).then_inc(q, 16),
                reads=[("dest_d", sq)], writes=[("lg16",)], dma_group="dstld")
            P.op("tensor", lambda e: e.matmul(
                self.ps_aux[:, 0:16], lg16[:, :], self.ident_f[0:16, 0:16],
                start=True, stop=True, is_transpose=True),
                reads=[("lg16",), ("ident_f",)], writes=[("ps_aux", "di")],
                tag="destT")
            P.op("vector", lambda e: e.tensor_copy(dintf[:, :],
                                                   self.ps_aux[:, 0:16]),
                 reads=[("ps_aux", "di")], writes=[("dintf",)])
            self.pad("vector")
            P.op("vector", lambda e: e.tensor_copy(dint[:, :], dintf[:, :]),
                 reads=[("dintf",)], writes=[("dint",)])
            self.pad("vector")
            for c in range(NT):
                P.op("gpsimd", lambda e, q, c=c, sq=sq: e.indirect_dma_start(
                    dap(s["tok_d"], 0, (1, NSEQ * 256), (1, 1)),
                    IndirectOffsetOnAxis(dint[:, c:c + 1], 0),
                    tokid[:, 16 * sq + c:16 * sq + c + 1], None,
                    bounds_check=NSEQ * 256 - 1, oob_is_err=False).then_inc(q, 16),
                    reads=[("dint",), ("tokid",)], writes=[("tok_d", sq)],
                    dma_group="tksc")
        self.P.join(self.junk)

        P.op("sync", lambda e, q: e.dma_start(
            tok_sb[:, :], dap(s["tok_d"], 0, (1, 128), (128, 4)),
            allow_slow_non_contiguous=True).then_inc(q, 16),
            reads=[("tok_d", 0), ("tok_d", 1)], writes=[("tok_sb",)],
            dma_group="tkld")
        for c in range(4):
            P.op("gpsimd", lambda e, q, c=c: e.indirect_dma_start(
                dump[:, :], None,
                s["h_full"].ap(), IndirectOffsetOnAxis(tok_sb[:, c:c + 1], 0),
                bounds_check=NSEQ * S - 1, oob_is_err=False).then_inc(q, 16),
                reads=[("tok_sb",)] + [("h_full", sq2, tt)
                                       for sq2 in range(2) for tt in range(NT)],
                writes=[("dump",)], dma_group="fxg")
            P.op("vector", lambda e: e.tensor_copy(fxc16[:, :], dump[:, :]),
                 reads=[("dump",)], writes=[("fxc16",)], tag="fx_cast")
            for jg in range(4):
                base = 1024 * (jg % 2)
                def tr(e, jg=jg, base=base):
                    for j4 in range(4):
                        j = 4 * jg + j4
                        ins = e.matmul(
                            self.ps_t[:, base + 128 * j4:base + 128 * (j4 + 1)],
                            fxc16[:, 128 * j:128 * (j + 1)],
                            self.ident_b[:, :], start=(j4 == 0),
                            stop=True, is_transpose=True)
                    return ins
                P.op("tensor", tr, reads=[("fxc16",), ("ident_b",)],
                     writes=[("ps_t", jg % 2)], tag="fx_tr")
                P.op("scalar", lambda e, jg=jg, base=base, c=c: e.copy(
                    dap(fxT, (4 * jg) * 512 + 128 * c, (NC * 512, 128), (512, 4), (1, 128)),
                    self.ps_t[:, base:base + 512]),
                    reads=[("ps_t", jg % 2)], writes=[("fxT", jg, c)])
        self.P.join(self.junk)
        for g in range(16):
            wb = wband[g % 2]
            def w1load(e, q, g=g, wb=wb):
                for j in range(NC):
                    e.dma_start(
                        wb[:, j, :],
                        dap(d["w1"], (128 * j) * 8192 + 512 * g,
                            (8192, 128), (1, 512))).then_inc(q, 16)
            P.op("gpsimd", w1load, writes=[("w1b", g % 2)],
                 dma_group=f"w1b{g % 2}", n_dma=NC)
            for mm2 in range(4):
                fc = 4 * g + mm2
                def gmm(e, mm2=mm2, wb=wb):
                    for k in range(NC):
                        ins = e.matmul(
                            self.ps_big[:, 512 * mm2:512 * (mm2 + 1)],
                            wb[:, k, 128 * mm2:128 * (mm2 + 1)],
                            fxT[:, k, :],
                            start=(k == 0), stop=(k == NC - 1))
                    return ins
                P.op("tensor", gmm,
                     reads=[("w1b", g % 2)] + [("fxT", jg2, cc)
                                               for jg2 in range(4) for cc in range(4)],
                     writes=[("ps_big", mm2)], tag="g1mm")
                self.copy_col(b1s[:, fc:fc + 1], ("b1s",), "t2")
                self.pad("vector")
                P.op("scalar", lambda e, fc=fc, mm2=mm2: e.activation(
                    hh[:, fc, :], self.ps_big[:, 512 * mm2:512 * (mm2 + 1)],
                    AF.Gelu, bias=st["t2"][:, :], scale=1.0),
                    reads=[("ps_big", mm2), ("st", "t2")],
                    writes=[("hh", fc)], tag="gelu")
        self.P.join(self.junk)
        for g in range(16):
            wb = w2band[g % 2]
            def w2load(e, q, g=g, wb=wb):
                for j in range(64):
                    e.dma_start(
                        wb[:, j, :],
                        dap(d["w2"], (128 * j) * 2048 + 128 * g,
                            (2048, 128), (1, 128))).then_inc(q, 16)
            P.op("gpsimd", w2load, writes=[("w2b", g % 2)],
                 dma_group=f"w2b{g % 2}", n_dma=64)
            def g2mm(e, g=g, wb=wb):
                for k in range(64):
                    ins = e.matmul(
                        self.ps_big[:, 512 * (g % 2):512 * (g % 2) + 512],
                        wb[:, k, :], hh[:, k, :],
                        start=(k == 0), stop=(k == 63))
                return ins
            P.op("tensor", g2mm,
                 reads=[("w2b", g % 2)] + [("hh", fc2) for fc2 in range(64)],
                 writes=[("ps_big", g % 2)], tag="g2mm")
            self.copy_col(b2s[:, g:g + 1], ("b2s",), "t2")
            self.pad("vector")
            P.op("vector", lambda e, g=g: e.scalar_tensor_tensor(
                out_l[:, g, :], self.ps_big[:, 512 * (g % 2):512 * (g % 2) + 512],
                1.0, st["t2"].broadcast_to([128, 512])[:, :],
                op0=ALU.mult, op1=ALU.add),
                reads=[("ps_big", g % 2), ("st", "t2")],
                writes=[("out_l", g)], tag="g2evac")
        self.P.join(self.junk)
        for uu in range(4):
            P.op("gpsimd", lambda e, q, uu=uu: e.indirect_dma_start(
                rwc[:, uu:uu + 1], None,
                dap(s["rw_d"], 0, (1, NSEQ * S), (1, 1)),
                IndirectOffsetOnAxis(tok_sb[:, uu:uu + 1], 0),
                bounds_check=NSEQ * S - 1, oob_is_err=False).then_inc(q, 16),
                reads=[("tok_sb",), ("rw_d", 0), ("rw_d", 1)],
                writes=[("rwc", uu)], dma_group="rwg")
        for uu in range(4):
            def otr(e, uu=uu):
                for j in range(NC):
                    ins = e.matmul(
                        self.ps_big[:, 128 * j:128 * (j + 1)],
                        out_l[:, j, 128 * uu:128 * (uu + 1)],
                        self.ident_f[:, :], start=(j % 4 == 0),
                        stop=True, is_transpose=True)
                return ins
            P.op("tensor", otr,
                 reads=[("out_l", fc2) for fc2 in range(16)] + [("ident_f",)],
                 writes=[("ps_big", 0), ("ps_big", 1), ("ps_big", 2),
                         ("ps_big", 3)], tag="o_tr")
            self.copy_col(rwc[:, uu:uu + 1], ("rwc", uu), "t2")
            self.pad("vector")
            P.op("vector", lambda e: e.tensor_scalar(
                dump[:, :], self.ps_big[:, :], st["t2"][:, :], None, op0=ALU.mult),
                reads=[("ps_big", 0), ("ps_big", 1), ("ps_big", 2), ("ps_big", 3),
                       ("st", "t2")],
                writes=[("dump",)], tag="o_scale")
            self.pad("vector")
            P.op("gpsimd", lambda e, q, uu=uu: e.indirect_dma_start(
                s["h_full"].ap(), IndirectOffsetOnAxis(tok_sb[:, uu:uu + 1], 0),
                dump[:, :], None,
                bounds_check=NSEQ * S - 1, oob_is_err=False,
                compute_op=ALU.add).then_inc(q, 16),
                reads=[("dump",), ("tok_sb",)] + [("h_full", sq2, tt)
                                                  for sq2 in range(2)
                                                  for tt in range(NT)],
                writes=[("h_sc", uu)] + [("h_full", sq2, tt)
                                         for sq2 in range(2) for tt in range(NT)],
                dma_group="hsc")
        self.P.join(self.junk)
        for sq in range(NSEQ):
            for t in range(NT):
                xres = ("xt", t % 2)
                xt = xa if t % 2 == 0 else xb
                P.op("sync", lambda e, q, sq=sq, t=t, xt=xt: e.dma_start(
                    xt[:, :], src.ap()[sq, 128 * t:128 * (t + 1), :]).then_inc(q, 16),
                    writes=[xres], dma_group=f"g5x{t % 2}")
                P.op("sync", lambda e, q, sq=sq, t=t: e.dma_start(
                    yout[:, :],
                    s["h_full"].ap()[sq * S + 128 * t:sq * S + 128 * (t + 1), :]).then_inc(q, 16),
                    reads=[("h_full", sq, t)],
                    writes=[("yout", 0)], dma_group=f"g5y{t % 2}")
                self.resid_store(xres, xt, dump, ("dump",), [("yout", 0)],
                                 yout, dst, sq, t, f"res3{t % 2}")
        self.final_groups.add("res30")
        self.final_groups.add("res31")


def build_nc(input_arrays, phases=("l0_attn", "l0_ffn", "l1_attn", "l1_ffn"),
             dbg=False):
    nc = bass.Bass("TRN2", target_bir_lowering=False)
    ctx = ExitStack()
    mk = MK(nc, ctx, phases, dbg=dbg)
    mk.build({k: (v.shape, v.dtype.type if hasattr(v.dtype, 'type') else v.dtype)
              for k, v in input_arrays.items()})
    ctx.close()
    return nc


def make_in_maps(inputs):
    c = host_constants()
    w = relayout_weights(inputs)
    x = np.asarray(inputs["L0_attn_x"], dtype=np.float32)
    base = dict(c)
    base.update(w)
    in_maps = []
    for core in range(8):
        im = dict(base)
        im["L0_attn_x"] = np.ascontiguousarray(x[2 * core:2 * core + 2])
        in_maps.append(im)
    return in_maps


def kernel(**inputs):
    in_maps = make_in_maps(inputs)
    nc = build_nc(in_maps[0])
    res = run_bass_kernel_spmd(nc, in_maps, core_ids=list(range(8)))
    return np.concatenate([r["x_out"] for r in res.results], axis=0)
